# version 07
# baseline (speedup 1.0000x reference)
import math
import contextlib
import collections
import numpy as np
import ml_dtypes

import concourse.bass as bass
import concourse.mybir as mybir
from concourse.bass_utils import run_bass_kernel_spmd
from concourse.aot_env import allow_ici_all_to_all

F32 = mybir.dt.float32
BF16 = mybir.dt.bfloat16
AF = mybir.ActivationFunctionType
OP = mybir.AluOpType
AX = mybir.AxisListType
BF = ml_dtypes.bfloat16

T = 8192
D = 2048
NH = 16
HD = 128
H4 = 8192
TOWN = 1024
NTT = 8
NEC = 16
ISQ = 1.0 / math.sqrt(128.0)
RMS_EPS = 1.1920929e-07


class _FakeInst:
    def then_inc(self, *a, **k):
        return self


class _FakeEng:
    def __getattr__(self, name):
        def f(*a, **k):
            return _FakeInst()
        return f


_FAKE = _FakeEng()


class P:

    def __init__(self, nc, sems, which, engine):
        self.nc = nc
        self.sems = sems
        self.which = which
        self.engine = engine
        self.cnt = collections.defaultdict(int)

    def eng(self, name):
        return self.engine if name == self.which else _FAKE

    def inc(self, inst, sem, n=1):
        self.cnt[sem] += n
        if inst is not None and not isinstance(inst, _FakeInst):
            inst.then_inc(self.sems[sem], n)
        return self.cnt[sem]

    def val(self, sem):
        return self.cnt[sem]

    def wait(self, engname, sem, val=None):
        if engname == self.which:
            v = self.cnt[sem] if val is None else val
            if v > 0:
                self.engine.wait_ge(self.sems[sem], v)


def _rope_tables(positions):
    inv_freq = 1.0 / (10000.0 ** (np.arange(0, HD, 2, dtype=np.float64) / HD))
    ang = positions[:, None].astype(np.float64) * inv_freq[None, :]
    cos = np.cos(ang).T
    sin = np.sin(ang).T
    C = np.empty((HD, len(positions)), dtype=np.float32)
    S = np.empty((HD, len(positions)), dtype=np.float32)
    C[0::2, :] = cos
    C[1::2, :] = cos
    S[0::2, :] = -sin
    S[1::2, :] = sin
    return C, S


def _diag_masks():
    m = np.zeros((128, 4 * 512), dtype=np.float32)
    for blk in range(4):
        for kt in range(128):
            q0 = 128 * blk + kt
            if q0 < 512:
                m[kt, blk * 512 + q0:(blk + 1) * 512] = 1.0
    return m.astype(BF)


class _FakeAP:
    shape = (128, 1024)

    def __getitem__(self, k):
        return self

    def rearrange(self, *a, **kw):
        return self

    def ap(self):
        return self

    def opt(self):
        return self


class _FakeCtx:
    def __init__(self, val):
        self.val = val

    def __enter__(self):
        return self.val

    def __exit__(self, *a):
        return False


class _RecInst:
    def __init__(self, stream, idx):
        self.stream = stream
        self.idx = idx

    def then_inc(self, sem, n=1):
        self.stream[self.idx]["incs"].append((sem, n))
        return self


class _RecEngine:
    def __init__(self, streams, name):
        self._streams = streams
        self._name = name
        self._pw = []

    def wait_ge(self, sem, val):
        self._pw.append((sem, val))

    def __getattr__(self, opname):
        if opname.startswith("_"):
            raise AttributeError(opname)

        def f(*a, **kw):
            st = self._streams[self._name]
            st.append({"op": opname, "waits": list(self._pw), "incs": []})
            self._pw = []
            return _RecInst(st, len(st) - 1)
        return f


class _FakeBlock:
    def __init__(self, streams):
        self._streams = streams

    def __enter__(self):
        return self

    def __exit__(self, *a):
        return False

    def __getattr__(self, name):
        if name.startswith("_"):
            raise AttributeError(name)

        def starter(f):
            f(_RecEngine(self._streams, name))
        return starter


class _FakeNC:
    def __init__(self):
        self.streams = {n: [] for n in
                        ["sync", "gpsimd", "scalar", "vector", "tensor"]}

    def dram_tensor(self, *a, **k):
        return _FakeAP()

    def sbuf_tensor(self, *a, **k):
        return _FakeCtx(_FakeAP())

    def psum_tensor(self, *a, **k):
        return _FakeCtx(_FakeAP())

    def semaphore(self, name=None):
        return _FakeCtx(name)

    def Block(self, *a, **k):
        return _FakeBlock(self.streams)


DEBUG = False


def build_program(sim=False):
    nc = _FakeNC() if sim else bass.Bass("TRN2", target_bir_lowering=False)

    x_in = nc.dram_tensor("x", [TOWN, D], F32, kind="ExternalInput")
    wqkvT = nc.dram_tensor("wqkvT", [D, 3 * D], F32, kind="ExternalInput")
    wprojT = nc.dram_tensor("wprojT", [D, D], F32, kind="ExternalInput")
    gu_shard = nc.dram_tensor("gu_shard", [D, D], F32, kind="ExternalInput")
    wp_shard = nc.dram_tensor("wp_shard", [1024, D], F32, kind="ExternalInput")
    cosk_in = nc.dram_tensor("cosk", [HD, TOWN], F32, kind="ExternalInput")
    sink_in = nc.dram_tensor("sink", [HD, TOWN], F32, kind="ExternalInput")
    dmask_in = nc.dram_tensor("dmask", [128, 4 * 512], BF16, kind="ExternalInput")
    iden_bf_in = nc.dram_tensor("iden_bf", [128, 128], BF16, kind="ExternalInput")
    iden_f32_in = nc.dram_tensor("iden_f32", [128, 128], F32, kind="ExternalInput")
    gmat_in = nc.dram_tensor("gmat", [128, 2], BF16, kind="ExternalInput")
    gmat2_in = nc.dram_tensor("gmat2", [2, 128], BF16, kind="ExternalInput")
    ones128_in = nc.dram_tensor("ones128c", [128, 1], BF16, kind="ExternalInput")
    ones1x_in = nc.dram_tensor("ones1x", [1, 128], BF16, kind="ExternalInput")

    out_t = nc.dram_tensor("out", [TOWN, D], F32, kind="ExternalOutput")
    dbg = {}
    if DEBUG:
        for nm, shp, dt_ in [("dbg_xn", [128, NEC * TOWN], BF16),
                             ("dbg_y", [128, NH * TOWN], BF16),
                             ("dbg_s", [128, NEC * TOWN], BF16),
                             ("dbg_g", [128, NEC * TOWN], BF16),
                             ("dbg_nb", [128, 2], F32),
                             ("dbg_ir", [128, TOWN], BF16),
                             ("dbg_h", [H4, TOWN], BF16),
                             ("dbg_kr", [8 * 258, TOWN], BF16),
                             ("dbg_vr", [8 * TOWN, 256], BF16),
                             ("dbg_qr", [8 * 258, TOWN], BF16)]:
            dbg[nm] = nc.dram_tensor(nm, shp, dt_, kind="ExternalOutput")
        for nm, shp, dt_ in [("dbg_fw0", [8 * 3072, 512], BF16),
                             ("dbg_p0", [128, 1024], BF16),
                             ("dbg_y0", [128, 512], BF16),
                             ("dbg_yu", [128, 512], BF16),
                             ("dbg_rs", [1, 512], BF16)]:
            dbg[nm] = nc.dram_tensor(nm, shp, dt_, kind="ExternalOutput")

    k_src = nc.dram_tensor("k_src", [8 * 258, TOWN], BF16)
    q_src = nc.dram_tensor("q_src", [8 * 258, TOWN], BF16)
    v_src = nc.dram_tensor("v_src", [8 * TOWN, 256], BF16)
    y_src = nc.dram_tensor("y_src", [8 * 256, TOWN], BF16)
    k_recv = nc.dram_tensor("k_recv", [8 * 258, TOWN], BF16)
    q_recv = nc.dram_tensor("q_recv", [8 * 258, TOWN], BF16)
    v_recv = nc.dram_tensor("v_recv", [8 * TOWN, 256], BF16)
    y_recv = nc.dram_tensor("y_recv", [8 * 256, TOWN], BF16)
    fw_src = [nc.dram_tensor(f"fw_src{p}", [3072, 512], BF16) for p in range(4)]
    fw_dst = [nc.dram_tensor(f"fw_dst{p}", [8 * 3072, 512], BF16) for p in range(4)]
    h_dram = nc.dram_tensor("h_dram", [H4, TOWN], BF16)

    ctx = contextlib.ExitStack()
    E = ctx.enter_context

    sem_names = (["qS", "qG", "qA", "eT", "eV", "eS",
                  "cK", "cV", "cQ", "cY"] + [f"cF{w}" for w in range(4)])
    sems = {n: E(nc.semaphore(n)) for n in sem_names}

    xn_fm = E(nc.sbuf_tensor("xn_fm", [128, NEC * TOWN], BF16))
    y_all = E(nc.sbuf_tensor("y_all", [128, NH * TOWN], BF16))
    wbuf = [E(nc.sbuf_tensor(f"wbuf{j}", [128, 2048], BF16)) for j in range(4)]
    xf = [E(nc.sbuf_tensor(f"xf{j}", [128, D], F32)) for j in range(3)]
    xnt = E(nc.sbuf_tensor("xnt", [128, D], BF16))
    ropeC = E(nc.sbuf_tensor("ropeC", [128, TOWN], F32))
    ropeS = E(nc.sbuf_tensor("ropeS", [128, TOWN], F32))
    dmask = E(nc.sbuf_tensor("dmask_sb", [128, 4 * 512], BF16))
    krt = E(nc.sbuf_tensor("krt", [128, TOWN], BF16))
    vfm = E(nc.sbuf_tensor("vfm", [128, TOWN], BF16))
    vown = E(nc.sbuf_tensor("vown", [128, TOWN], BF16))
    pful = [E(nc.sbuf_tensor(f"pful{j}", [128, 1024], BF16)) for j in range(2)]
    sq_tmp = E(nc.sbuf_tensor("sq_tmp", [128, TOWN], BF16))
    sgbuf = E(nc.sbuf_tensor("sgbuf", [128, TOWN], BF16))
    hbuf = [E(nc.sbuf_tensor(f"hbuf{j}", [128, TOWN], BF16)) for j in range(2)]
    hsqb = E(nc.sbuf_tensor("hsqb", [128, TOWN], BF16))
    invr_rep = E(nc.sbuf_tensor("invr_rep", [128, TOWN], BF16))
    gt32 = E(nc.sbuf_tensor("gt32", [128, TOWN], F32))
    ysb_tmp = E(nc.sbuf_tensor("ysb_tmp", [128, 512], BF16))
    iden_bf = E(nc.sbuf_tensor("iden_bf_sb", [128, 128], BF16))
    iden_f32 = E(nc.sbuf_tensor("iden_f32_sb", [128, 128], F32))
    gmat = E(nc.sbuf_tensor("gmat_sb", [128, 2], BF16))
    gmat2 = E(nc.sbuf_tensor("gmat2_sb", [2, 128], BF16))
    ones128 = E(nc.sbuf_tensor("ones128_sb", [128, 1], BF16))
    ones1x = E(nc.sbuf_tensor("ones1x_sb", [1, 128], BF16))
    negB = E(nc.sbuf_tensor("negB", [128, 2], F32))
    stat = E(nc.sbuf_tensor("stat", [128, 8], F32))
    knrow = E(nc.sbuf_tensor("knrow", [1, 4096], BF16))
    kn_stage = E(nc.sbuf_tensor("kn_stage", [1, TOWN], BF16))
    invr_row = E(nc.sbuf_tensor("invr_row", [1, TOWN], F32))
    invr_bf = E(nc.sbuf_tensor("invr_bf", [1, TOWN], BF16))
    rsum_sb = E(nc.sbuf_tensor("rsum_sb", [1, 512], F32))
    rs_bf = E(nc.sbuf_tensor("rs_bf", [1, 512], BF16))
    km_sb = E(nc.sbuf_tensor("km_sb", [1, 8], F32))
    bp_sb = E(nc.sbuf_tensor("bp_sb", [1, 8], F32))
    bn_bf = E(nc.sbuf_tensor("bn_bf", [1, 8], BF16))
    qthr = E(nc.sbuf_tensor("qthr", [2, 512], BF16))
    sgf = E(nc.sbuf_tensor("sgf", [128, 128], F32))
    jnk = E(nc.sbuf_tensor("jnk", [128, 2048], F32))

    with contextlib.ExitStack() as pA:
        psA = pA.enter_context(nc.psum_tensor("psA", [128, 1024], F32))
        psB = pA.enter_context(nc.psum_tensor("psB", [128, 1024], F32))
        vt_ps = pA.enter_context(nc.psum_tensor("vt_ps", [128, 1024], BF16))
        kn_ps = pA.enter_context(nc.psum_tensor("kn_ps", [1, 1024], F32))
        bB_ps = pA.enter_context(nc.psum_tensor("bB_ps", [128, 2], F32))
    with contextlib.ExitStack() as pB:
        scps = [pB.enter_context(nc.psum_tensor(f"sc{j}", [128, 1024], F32))
                for j in range(2)]
        yT = [pB.enter_context(nc.psum_tensor(f"yT{j}", [128, 512], F32))
              for j in range(2)]
        smp = [pB.enter_context(nc.psum_tensor(f"smp{j}", [128, 512], F32))
               for j in range(2)]
    with contextlib.ExitStack() as pD:
        gA = pD.enter_context(nc.psum_tensor("gA", [128, 1024], F32))
        uB = pD.enter_context(nc.psum_tensor("uB", [128, 1024], F32))
        u45 = [pD.enter_context(nc.psum_tensor(f"u4{j}", [128, 512], F32))
               for j in range(2)]
        sq67 = [pD.enter_context(nc.psum_tensor(f"sq6{j}", [1, 512], F32))
                for j in range(2)]
    with contextlib.ExitStack() as pC:
        prA = pC.enter_context(nc.psum_tensor("prA", [128, 1024], F32))
        prB = pC.enter_context(nc.psum_tensor("prB", [128, 1024], F32))
    with contextlib.ExitStack() as pE:
        fp = [pE.enter_context(nc.psum_tensor(f"fp{j}", [128, 1024], F32))
              for j in range(4)]
    with contextlib.ExitStack() as pF:
        tp = [pF.enter_context(nc.psum_tensor(f"tp{j}", [128, 512], F32))
              for j in range(2)]

    fsubs = ([("K", h) for h in range(NH)] + [("Q", h) for h in range(NH)]
             + [("V", h) for h in range(NH)])
    NSUB = len(fsubs)

    def fcol(kind, h):
        return {"Q": 0, "K": D, "V": 2 * D}[kind] + 128 * h

    swap_mask = []
    for jj in range(16):
        swap_mask += [2 * jj + 1, 2 * jj]

    qtiles = []
    for w in range(4):
        for ch in range(16):
            qtiles.append((0, ch, w))
        for ch in range(8):
            qtiles.append((1, ch, w))

    def emit(p: P):
        t = p.eng("tensor")
        v = p.eng("vector")
        s = p.eng("scalar")
        sy = p.eng("sync")
        gp = p.eng("gpsimd")

        def vsync(inst):
            m = p.inc(inst, "eV", 1)
            p.wait("vector", "eV", m)
            return m

        def vdup(emit):
            emit()
            return p.inc(emit(), "eV", 1)

        def sdup(emit):
            emit()
            return p.inc(emit(), "eS", 1)

        for src, dst in [
            (dmask_in, dmask), (iden_bf_in, iden_bf), (iden_f32_in, iden_f32),
            (gmat_in, gmat), (gmat2_in, gmat2), (ones128_in, ones128),
            (ones1x_in, ones1x), (cosk_in, ropeC), (sink_in, ropeS),
        ]:
            p.inc(sy.dma_start(out=dst[:, :], in_=src[:, :]), "qS", 16)
        const_ld = p.val("qS")

        ln_ev = []
        ln_evac = []
        for tt in range(NTT):
            b = tt % 2
            if tt >= 2:
                p.wait("sync", "eV", ln_ev[tt - 2])
            p.inc(sy.dma_start(out=xf[b][:, :], in_=x_in[128 * tt:128 * (tt + 1), :]),
                  "qS", 16)
            xld = p.val("qS")
            p.wait("vector", "qS", xld)
            v.tensor_tensor(out=xf[2][:, :], in0=xf[b][:, :], in1=xf[b][:, :],
                            op=OP.mult)
            vsync(v.tensor_reduce(out=stat[:, 0:1], in_=xf[b][:, :], axis=AX.X,
                                  op=OP.add))
            vsync(v.tensor_reduce(out=stat[:, 1:2], in_=xf[2][:, :], axis=AX.X,
                                  op=OP.add))
            vsync(v.tensor_scalar(out=stat[:, 2:3], in0=stat[:, 0:1],
                                  scalar1=1.0 / D, scalar2=None, op0=OP.mult))
            vsync(v.tensor_scalar(out=stat[:, 3:4], in0=stat[:, 1:2],
                                  scalar1=1.0 / D, scalar2=None, op0=OP.mult))
            vsync(v.tensor_tensor(out=stat[:, 4:5], in0=stat[:, 2:3],
                                  in1=stat[:, 2:3], op=OP.mult))
            vsync(v.tensor_tensor(out=stat[:, 5:6], in0=stat[:, 3:4],
                                  in1=stat[:, 4:5], op=OP.subtract))
            vsync(v.tensor_scalar(out=stat[:, 6:7], in0=stat[:, 5:6],
                                  scalar1=1e-5, scalar2=1e-6,
                                  op0=OP.add, op1=OP.max))
            vdup(lambda: v.reciprocal(out=stat[:, 7:8], in_=stat[:, 6:7]))
            p.wait("scalar", "eV", p.val("eV"))
            sdup(lambda: s.activation(out=stat[:, 6:7], in_=stat[:, 7:8],
                                      func=AF.Sqrt))
            p.wait("vector", "eS", p.val("eS"))
            inst = v.tensor_scalar(out=xnt[:, :], in0=xf[b][:, :],
                                   scalar1=stat[:, 2:3], scalar2=stat[:, 6:7],
                                   op0=OP.subtract, op1=OP.mult)
            ln_ev.append(p.inc(inst, "eV", 1))
            p.wait("tensor", "eV", p.val("eV"))
            if tt == 0:
                p.wait("tensor", "qS", const_ld)
            for half in range(2):
                if ln_evac:
                    p.wait("tensor", "eV", ln_evac[-1])
                for j in range(8):
                    ec = half * 8 + j
                    inst = t.transpose(out=vt_ps[:, j * 128:(j + 1) * 128],
                                       in_=xnt[:, ec * 128:(ec + 1) * 128],
                                       identity=iden_bf[:, :])
                p.inc(inst, "eT", 1)
                p.wait("vector", "eT", p.val("eT"))
                dst = xn_fm[:, :].rearrange("p (ec tt2 c) -> p ec tt2 c",
                                            ec=NEC, tt2=NTT)
                inst = v.tensor_copy(
                    out=dst[:, half * 8:(half + 1) * 8, tt, :],
                    in_=vt_ps[:, :].rearrange("p (j c) -> p j c", j=8))
                ln_evac.append(p.inc(inst, "eV", 1))
        ln_all = p.val("eV")
        if DEBUG:
            p.wait("sync", "eV", ln_all)
            p.inc(sy.dma_start(out=dbg["dbg_xn"][:, :], in_=xn_fm[:, :]), "qS", 16)

        wq_ld = [None] * NSUB
        et_sub = [None] * NSUB
        ev_sub = [None] * NSUB
        stg_wr = {}
        qn_ev = {}

        for fs, (kind, h) in enumerate(fsubs):
            pair, l = divmod(h, 2)
            fb = fcol(kind, h)
            if fs >= 4:
                p.wait("gpsimd", "eT", et_sub[fs - 4])
            src = wqkvT[:, fb:fb + 128].rearrange("(ec p2) f -> p2 ec f", p2=128)
            inst = gp.dma_start(
                out=wbuf[fs % 4][:, :].rearrange("p2 (ec f) -> p2 ec f", ec=16),
                in_=src)
            wq_ld[fs] = p.inc(inst, "qG", 16)
            ps = psA if fs % 2 == 0 else psB
            if fs >= 2:
                p.wait("tensor", "eV", ev_sub[fs - 2])
            if fs == 0:
                p.wait("tensor", "eV", ln_all)
            p.wait("tensor", "qG", wq_ld[fs])
            for ec in range(NEC):
                lhsT = wbuf[fs % 4][:, ec * 128:(ec + 1) * 128]
                for th in range(2):
                    inst = t.matmul(
                        ps[:, th * 512:(th + 1) * 512], lhsT,
                        xn_fm[:, ec * 1024 + th * 512:ec * 1024 + (th + 1) * 512],
                        start=(ec == 0), stop=(ec == NEC - 1))
            et_sub[fs] = p.inc(inst, "eT", 1)
            p.wait("vector", "eT", et_sub[fs])
            if kind in ("K", "Q"):
                if fs >= 1 and (fs - 1) in stg_wr:
                    p.wait("vector", "qG", stg_wr[fs - 1])
                v.stream_shuffle(out=xf[2][:, 0:1024], in_=ps[:, :], mask=swap_mask)
                v.tensor_tensor(out=gt32[:, :], in0=ps[:, :], in1=ropeC[:, :],
                                op=OP.mult)
                v.tensor_tensor(out=xf[2][:, 1024:2048], in0=xf[2][:, 0:1024],
                                in1=ropeS[:, :], op=OP.mult)
                v.tensor_tensor(out=krt[:, :], in0=gt32[:, :],
                                in1=xf[2][:, 1024:2048], op=OP.add)
                inst = v.tensor_tensor(out=krt[:, :], in0=gt32[:, :],
                                       in1=xf[2][:, 1024:2048], op=OP.add)
                ev_sub[fs] = p.inc(inst, "eV", 1)
                p.wait("scalar", "eV", ev_sub[fs])
                s.activation(out=sq_tmp[:, :], in_=krt[:, :], func=AF.Square)
                inst = s.activation(out=sq_tmp[:, :], in_=krt[:, :], func=AF.Square)
                sqm = p.inc(inst, "eS", 1)
                p.wait("tensor", "eS", sqm)
                p.wait("tensor", "eV", p.val("eV"))
                for th in range(2):
                    inst = t.matmul(kn_ps[0:1, th * 512:(th + 1) * 512],
                                    ones128[:, :],
                                    sq_tmp[:, th * 512:(th + 1) * 512],
                                    start=True, stop=True)
                knm = p.inc(inst, "eT", 1)
                p.wait("vector", "eT", knm)
                knc = vdup(lambda: v.tensor_copy(out=kn_stage[0:1, :],
                                                 in_=kn_ps[0:1, :]))
                dstt = k_src if kind == "K" else q_src
                p.wait("gpsimd", "eV", ev_sub[fs])
                inst = gp.dma_start(
                    out=dstt[258 * pair + 128 * l:258 * pair + 128 * (l + 1), :],
                    in_=krt[:, :])
                p.inc(inst, "qG", 16)
                p.wait("gpsimd", "eV", knc)
                inst = gp.dma_start(
                    out=dstt[258 * pair + 256 + l:258 * pair + 257 + l, :],
                    in_=kn_stage[0:1, :])
                stg_wr[fs] = p.inc(inst, "qG", 16)
            else:
                v.tensor_copy(out=vfm[:, :], in_=ps[:, :])
                inst = v.tensor_copy(out=vfm[:, :], in_=ps[:, :])
                vc = p.inc(inst, "eV", 1)
                ev_sub[fs] = vc
                p.wait("tensor", "eV", vc)
                p.wait("tensor", "eV", p.val("eV"))
                for tb in range(8):
                    inst = t.transpose(out=vt_ps[:, tb * 128:(tb + 1) * 128],
                                       in_=vfm[:, tb * 128:(tb + 1) * 128],
                                       identity=iden_bf[:, :])
                vt = p.inc(inst, "eT", 1)
                p.wait("vector", "eT", vt)
                if fs >= 1 and (fs - 1) in stg_wr:
                    p.wait("vector", "qG", stg_wr[fs - 1])
                v.tensor_copy(out=vown[:, :], in_=vt_ps[:, :])
                inst = v.tensor_copy(out=vown[:, :], in_=vt_ps[:, :])
                voc = p.inc(inst, "eV", 1)
                p.wait("gpsimd", "eV", voc)
                inst = gp.dma_start(
                    out=v_src[1024 * pair:1024 * (pair + 1),
                              128 * l:128 * (l + 1)]
                    .rearrange("(tb p2) d -> p2 tb d", p2=128),
                    in_=vown[:, :].rearrange("p2 (tb d) -> p2 tb d", tb=8))
                stg_wr[fs] = p.inc(inst, "qG", 16)
            if fs in (NH - 1, 2 * NH - 1, 3 * NH - 1):
                p.wait("gpsimd", "qG", stg_wr[fs])
                kindmap = {NH - 1: ("cK", k_src, k_recv),
                           2 * NH - 1: ("cQ", q_src, q_recv),
                           3 * NH - 1: ("cV", v_src, v_recv)}
                cs, srcten, dstten = kindmap[fs]
                with allow_ici_all_to_all():
                    inst = gp.collective_compute(
                        "AllToAll", OP.bypass, replica_groups=[list(range(8))],
                        ins=[srcten.ap().opt()], outs=[dstten.ap().opt()])
                p.inc(inst, cs, 1)

        qkv_eT = p.val("eT")
        qkv_eV = p.val("eV")

        knrow_red = 0
        for l in range(2):
            for which, csem, ten, cell0 in [(0, "cK", k_recv, 0),
                                            (1, "cQ", q_recv, 4)]:
                p.wait("scalar", csem, 1)
                for rnd in range(2):
                    if knrow_red:
                        p.wait("scalar", "eV", knrow_red)
                    for rr in range(4):
                        r = 4 * rnd + rr
                        inst = s.dma_start(
                            out=knrow[0:1, rr * 1024:(rr + 1) * 1024],
                            in_=ten[258 * r + 256 + l:258 * r + 257 + l, :])
                        p.inc(inst, "qA", 16)
                    p.wait("vector", "qA", p.val("qA"))
                    cell = km_sb[0:1, cell0 + 2 * l + rnd:cell0 + 2 * l + rnd + 1]
                    knrow_red = vsync(v.tensor_reduce(out=cell, in_=knrow[0:1, :],
                                                      axis=AX.X, op=OP.max))
                inst = v.tensor_tensor(
                    out=km_sb[0:1, cell0 + 2 * l:cell0 + 2 * l + 1],
                    in0=km_sb[0:1, cell0 + 2 * l:cell0 + 2 * l + 1],
                    in1=km_sb[0:1, cell0 + 2 * l + 1:cell0 + 2 * l + 2], op=OP.max)
                vsync(inst)
        for l in range(2):
            inst = v.tensor_tensor(out=bp_sb[0:1, l:l + 1],
                                   in0=km_sb[0:1, 2 * l:2 * l + 1],
                                   in1=km_sb[0:1, 4 + 2 * l:5 + 2 * l],
                                   op=OP.mult)
            vsync(inst)
        vdup(lambda: v.tensor_copy(out=bp_sb[0:1, 4:6], in_=bp_sb[0:1, 0:2]))
        p.wait("scalar", "eV", p.val("eV"))
        sdup(lambda: s.activation(out=bp_sb[0:1, 2:4], in_=bp_sb[0:1, 4:6],
                                  func=AF.Sqrt))
        p.wait("vector", "eS", p.val("eS"))
        vdup(lambda: v.tensor_scalar(out=bn_bf[0:1, 0:2], in0=bp_sb[0:1, 2:4],
                                     scalar1=-ISQ, scalar2=-60.0,
                                     op0=OP.mult, op1=OP.max))
        bnv = p.val("eV")
        p.wait("tensor", "eV", bnv)
        for l in range(2):
            inst = t.matmul(bB_ps[:, l:l + 1], ones1x[:, :],
                            bn_bf[0:1, l:l + 1], start=True, stop=True)
        bmm = p.inc(inst, "eT", 1)
        p.wait("vector", "eT", bmm)
        negB_ev = vdup(lambda: v.tensor_copy(out=negB[:, :], in_=bB_ps[:, :]))
        if DEBUG:
            p.wait("sync", "eV", negB_ev)
            p.inc(sy.dma_start(out=dbg["dbg_nb"][:, :], in_=negB[:, :]), "qS", 16)
            p.wait("sync", "cK", 1)
            p.inc(sy.dma_start(out=dbg["dbg_kr"][:, :], in_=k_recv[:, :]), "qS", 16)
            p.wait("sync", "cQ", 1)
            p.inc(sy.dma_start(out=dbg["dbg_qr"][:, :], in_=q_recv[:, :]), "qS", 16)
            p.wait("sync", "cV", 1)
            p.inc(sy.dma_start(out=dbg["dbg_vr"][:, :], in_=v_recv[:, :]), "qS", 16)

        fw_wr = [0] * 4
        q_hv = []
        prev_gs = 0
        for ti, (which, ch, w) in enumerate(qtiles):
            buf = wbuf[ti % 2]
            hat = wbuf[2 + ti % 2]
            srcT = gu_shard if which == 0 else wp_shard
            if ti < 2:
                p.wait("gpsimd", "eT", qkv_eT)
            else:
                p.wait("gpsimd", "eV", q_hv[ti - 2])
            inst = gp.dma_start(out=buf[:, 0:512],
                                in_=srcT[128 * ch:128 * (ch + 1),
                                         512 * w:512 * (w + 1)])
            ldv = p.inc(inst, "qG", 16)
            p.wait("scalar", "qG", ldv)
            if ti >= 1:
                p.wait("scalar", "eV", q_hv[ti - 1])
                p.wait("scalar", "eT", prev_gs)
            inst = s.activation(out=sq_tmp[:, 0:512], in_=buf[:, 0:512],
                                func=AF.Abs)
            absv = p.inc(inst, "eS", 1)
            inst = s.activation(out=sgbuf[:, 0:512], in_=buf[:, 0:512],
                                func=AF.Sign)
            sg = p.inc(inst, "eS", 1)
            p.wait("tensor", "eS", absv)
            if ti == 0:
                p.wait("tensor", "eV", qkv_eV)
            inst = t.matmul(psA[0:2, 0:512], gmat[:, :], sq_tmp[:, 0:512],
                            start=True, stop=True)
            gs = p.inc(inst, "eT", 1)
            prev_gs = gs
            p.wait("vector", "eT", gs)
            thv = vdup(lambda: v.tensor_scalar(out=qthr[:, :],
                                               in0=psA[0:2, 0:512],
                                               scalar1=0.5 / 64.0, scalar2=5e-9,
                                               op0=OP.mult, op1=OP.max))
            p.wait("tensor", "eV", thv)
            inst = t.matmul(psB[:, 0:512], gmat2[:, :], qthr[:, :],
                            start=True, stop=True)
            tr = p.inc(inst, "eT", 1)
            p.wait("vector", "eT", tr)
            p.wait("vector", "eS", sg)
            v.tensor_scalar(out=hsqb[:, 0:512], in0=sq_tmp[:, 0:512],
                            scalar1=2.0, scalar2=None, op0=OP.mult)
            v.tensor_tensor(out=hbuf[0][:, 0:512], in0=hsqb[:, 0:512],
                            in1=psB[:, 0:512], op=OP.is_gt)
            v.tensor_tensor(out=hbuf[1][:, 0:512], in0=hbuf[0][:, 0:512],
                            in1=sgbuf[:, 0:512], op=OP.mult)
            inst = v.tensor_tensor(out=hat[:, 0:512], in0=hbuf[1][:, 0:512],
                                   in1=psB[:, 0:512], op=OP.mult)
            hv = p.inc(inst, "eV", 1)
            q_hv.append(hv)
            p.wait("gpsimd", "eV", hv)
            row0 = 128 * ch if which == 0 else 2048 + 128 * ch
            inst = gp.dma_start(out=fw_src[w][row0:row0 + 128, :],
                                in_=hat[:, 0:512])
            fw_wr[w] = p.inc(inst, "qG", 16)
        quant_eV = p.val("eV")
        quant_qG = p.val("qG")
        for w in range(4):
            p.wait("gpsimd", "qG", fw_wr[w])
            inst = gp.collective_compute(
                "AllGather", OP.bypass, replica_groups=[list(range(8))],
                ins=[fw_src[w].ap().opt()], outs=[fw_dst[w].ap().opt()])
            p.inc(inst, f"cF{w}", 1)

        if DEBUG:
            p.wait("sync", "cF0", 1)
            p.inc(sy.dma_start(out=dbg["dbg_fw0"][:, :], in_=fw_dst[0][:, :]),
                  "qS", 16)

        exp_mk = {}
        pv_mk = {}
        ep_sc = {}
        ep_ev = {}
        gc_marks = []
        q_marks = {}
        yst_wr = {}
        gc = 0
        last_ep_ev = None
        for l in range(2):
            for pp in range(8):
                nbA = 4 * (2 * pp + 1)
                nbB = nbA + 4
                nch = nbB // 4
                if (l, pp) == (0, 0):
                    p.wait("sync", "cQ", 1)
                    p.wait("sync", "cK", 1)
                    p.wait("sync", "cV", 1)
                    p.wait("sync", "eV", quant_eV)
                    p.wait("sync", "qG", quant_qG)
                else:
                    prevkey = (l, pp - 1) if pp > 0 else (l - 1, 7)
                    p.wait("sync", "eT", q_marks[prevkey])
                inst = sy.dma_start(
                    out=wbuf[2][:, 0:1024],
                    in_=q_recv[258 * pp + 128 * l:258 * pp + 128 * (l + 1), :])
                qld = p.inc(inst, "qS", 16)
                for blk in range(nbB):
                    kc = blk // 4
                    bi = blk % 4
                    in_A = blk < nbA
                    sc = scps[blk % 2]
                    pb = pful[blk % 2]
                    if bi == 0:
                        if gc >= 2:
                            p.wait("sync", "eT", gc_marks[gc - 2])
                        inst = sy.dma_start(
                            out=wbuf[gc % 2][:, 0:512],
                            in_=k_recv[258 * (kc // 2) + 128 * l:
                                       258 * (kc // 2) + 128 * (l + 1),
                                       (kc % 2) * 512:(kc % 2) * 512 + 512])
                        p.inc(inst, "qS", 16)
                        inst = sy.dma_start(
                            out=wbuf[gc % 2][:, 512:1024]
                            .rearrange("p2 (tb d) -> p2 tb d", tb=4),
                            in_=v_recv[1024 * (kc // 2) + (kc % 2) * 512:
                                       1024 * (kc // 2) + (kc % 2) * 512 + 512,
                                       128 * l:128 * (l + 1)]
                            .rearrange("(tb p2) d -> p2 tb d", p2=128))
                        kvld = p.inc(inst, "qS", 16)
                        gc_ld = kvld
                    p.wait("tensor", "qS", gc_ld)
                    if blk == 0:
                        p.wait("tensor", "qS", qld)
                        if (l, pp) == (0, 0):
                            p.wait("tensor", "eV", quant_eV)
                    prev = None
                    if blk >= 2:
                        prev = (l, pp, blk - 2)
                    elif (l, pp) != (0, 0):
                        pk = (l, pp - 1) if pp > 0 else (l - 1, 7)
                        prev = (pk[0], pk[1], 4 * (2 * pk[1] + 2) - 2 + blk)
                    if prev is not None:
                        p.wait("tensor", "eS", exp_mk[prev])
                    klhs = wbuf[gc % 2][:, bi * 128:(bi + 1) * 128]
                    if in_A:
                        t.matmul(sc[:, 0:512], klhs, wbuf[2][:, 0:512],
                                 start=True, stop=True)
                    inst = t.matmul(sc[:, 512:1024], klhs, wbuf[2][:, 512:1024],
                                    start=True, stop=True)
                    qk = p.inc(inst, "eT", 1)
                    p.wait("scalar", "eT", qk)
                    if (l, pp, blk) == (0, 0, 0):
                        p.wait("scalar", "eV", negB_ev)
                    if prev is not None:
                        p.wait("scalar", "eT", pv_mk[prev])
                    if in_A:
                        inst = s.activation(out=pb[:, :], in_=sc[:, :],
                                            func=AF.Exp, scale=ISQ,
                                            bias=negB[:, l:l + 1])
                    else:
                        inst = s.activation(out=pb[:, 512:1024],
                                            in_=sc[:, 512:1024],
                                            func=AF.Exp, scale=ISQ,
                                            bias=negB[:, l:l + 1])
                    exp_mk[(l, pp, blk)] = p.inc(inst, "eS", 1)
                    mask_ev = None
                    mA = blk - (nbA - 4)
                    mB = blk - (nbB - 4)
                    if (in_A and 0 <= mA < 4) or 0 <= mB < 4:
                        p.wait("vector", "eS", exp_mk[(l, pp, blk)])
                        if in_A and 0 <= mA < 4:
                            inst = v.tensor_tensor(
                                out=pb[:, 0:512], in0=pb[:, 0:512],
                                in1=dmask[:, mA * 512:(mA + 1) * 512], op=OP.mult)
                        if 0 <= mB < 4:
                            inst = v.tensor_tensor(
                                out=pb[:, 512:1024], in0=pb[:, 512:1024],
                                in1=dmask[:, mB * 512:(mB + 1) * 512], op=OP.mult)
                        mask_ev = p.inc(inst, "eV", 1)
                    if mask_ev is not None:
                        p.wait("tensor", "eV", mask_ev)
                    else:
                        p.wait("tensor", "eS", exp_mk[(l, pp, blk)])
                    if blk == 0 and (l, pp) != (0, 0):
                        pk = (l, pp - 1) if pp > 0 else (l - 1, 7)
                        p.wait("tensor", "eV", ep_sc[pk + (0,)])
                        p.wait("tensor", "eV", ep_sc[pk + (1,)])
                        p.wait("tensor", "eV", ep_ev[pk + (0,)])
                        p.wait("tensor", "eV", ep_ev[pk + (1,)])
                    vlhs = wbuf[gc % 2][:, 512 + bi * 128:512 + (bi + 1) * 128]
                    if in_A:
                        t.matmul(yT[0][:, :], vlhs, pb[:, 0:512],
                                 start=(blk == 0), stop=(blk == nbA - 1))
                        t.matmul(smp[0][0:1, :], ones128[:, :], pb[:, 0:512],
                                 start=(blk == 0), stop=(blk == nbA - 1))
                    t.matmul(yT[1][:, :], vlhs, pb[:, 512:1024],
                             start=(blk == 0), stop=(blk == nbB - 1))
                    inst = t.matmul(smp[1][0:1, :], ones128[:, :],
                                    pb[:, 512:1024],
                                    start=(blk == 0), stop=(blk == nbB - 1))
                    pv_mk[(l, pp, blk)] = p.inc(inst, "eT", 1)
                    if DEBUG and (l, pp, blk) == (0, 0, 0):
                        p.wait("sync", "eT", pv_mk[(l, pp, blk)])
                        p.inc(sy.dma_start(out=dbg["dbg_p0"][:, :],
                                           in_=pful[0][:, :]), "qS", 16)
                    if bi == 3:
                        gc_marks.append(p.val("eT"))
                        gc += 1
                    if blk == nbB - 1:
                        q_marks[(l, pp)] = p.val("eT")
                    for ci in range(2):
                        if blk != (nbA - 1 if ci == 0 else nbB - 1):
                            continue
                        cglob = 2 * pp + ci
                        p.wait("vector", "eT", pv_mk[(l, pp, blk)])
                        vsync(v.reciprocal(out=rsum_sb[0:1, :],
                                           in_=smp[ci][0:1, :]))
                        rcp = vdup(lambda: v.tensor_copy(out=rs_bf[0:1, :],
                                                         in_=rsum_sb[0:1, :]))
                        p.wait("tensor", "eV", rcp)
                        inst = t.matmul(smp[ci][:, :], ones1x[:, :],
                                        rs_bf[0:1, :], start=True, stop=True)
                        rep = p.inc(inst, "eT", 1)
                        inst = v.tensor_copy(out=ysb_tmp[:, :],
                                             in_=yT[ci][:, :])
                        ep_sc[(l, pp, ci)] = p.inc(inst, "eV", 1)
                        p.wait("vector", "eT", rep)
                        if (l, cglob - 1) in yst_wr:
                            p.wait("vector", "qS", yst_wr[(l, cglob - 1)])
                        elif l == 1 and cglob == 0:
                            p.wait("vector", "qS", yst_wr[(0, 15)])
                        v.tensor_tensor(out=krt[:, 0:512],
                                        in0=ysb_tmp[:, :],
                                        in1=smp[ci][:, :], op=OP.mult)
                        inst = v.tensor_tensor(out=krt[:, 0:512],
                                               in0=ysb_tmp[:, :],
                                               in1=smp[ci][:, :], op=OP.mult)
                        ep_ev[(l, pp, ci)] = p.inc(inst, "eV", 1)
                        last_ep_ev = ep_ev[(l, pp, ci)]
                        if DEBUG and (l, pp, ci) == (0, 0, 0):
                            p.wait("sync", "eV", ep_sc[(l, pp, ci)])
                            p.inc(sy.dma_start(out=dbg["dbg_yu"][:, :],
                                               in_=ysb_tmp[:, :]), "qS", 16)
                            p.wait("sync", "eV", p.val("eV"))
                            p.inc(sy.dma_start(out=dbg["dbg_rs"][:, :],
                                               in_=rs_bf[0:1, :]), "qS", 16)
                        p.wait("sync", "eV", ep_ev[(l, pp, ci)])
                        inst = sy.dma_start(
                            out=y_src[256 * (cglob // 2) + 128 * l:
                                      256 * (cglob // 2) + 128 * (l + 1),
                                      (cglob % 2) * 512:(cglob % 2) * 512 + 512],
                            in_=krt[:, 0:512])
                        yst_wr[(l, cglob)] = p.inc(inst, "qS", 16)
                        if DEBUG and (l, pp, ci) == (0, 0, 0):
                            p.inc(sy.dma_start(out=dbg["dbg_y0"][:, :],
                                               in_=krt[:, 0:512]), "qS", 16)
        attn_eT = p.val("eT")
        attn_qS = p.val("qS")
        attn_eV = p.val("eV")
        attn_eS = p.val("eS")
        p.wait("gpsimd", "qS", attn_qS)
        with allow_ici_all_to_all():
            inst = gp.collective_compute(
                "AllToAll", OP.bypass, replica_groups=[list(range(8))],
                ins=[y_src.ap().opt()], outs=[y_recv.ap().opt()])
        p.inc(inst, "cY", 1)

        g_ld = {}
        g_et = {}
        sil_mk = {}
        h_ev = {}
        hq_ev = {}
        h_wr = {}
        sum_et = {}
        for gpi in range(16):
            w = gpi % 4
            r_g = gpi // 4
            r_u = 4 + gpi // 4
            for j in range(4):
                pi = 4 * gpi + j
                bslot = wbuf[(2 * pi) % 4]
                uslot = wbuf[(2 * pi + 1) % 4]
                p.wait("sync", f"cF{w}", 1)
                if pi == 0:
                    p.wait("sync", "eT", attn_eT)
                if pi >= 2:
                    p.wait("sync", "eT", g_et[pi - 2])
                inst = sy.dma_start(
                    out=bslot[:, :].rearrange("p2 (ec f) -> p2 ec f", ec=16),
                    in_=fw_dst[w][3072 * r_g:3072 * r_g + 2048,
                                  128 * j:128 * (j + 1)]
                    .rearrange("(ec p2) f -> p2 ec f", p2=128))
                p.inc(inst, "qS", 16)
                inst = sy.dma_start(
                    out=uslot[:, :].rearrange("p2 (ec f) -> p2 ec f", ec=16),
                    in_=fw_dst[w][3072 * r_u:3072 * r_u + 2048,
                                  128 * j:128 * (j + 1)]
                    .rearrange("(ec p2) f -> p2 ec f", p2=128))
                g_ld[pi] = p.inc(inst, "qS", 16)
                p.wait("tensor", "qS", g_ld[pi])
                if pi == 0:
                    p.wait("tensor", "eV", attn_eV)
                    p.wait("tensor", "eS", attn_eS)
                if pi >= 1:
                    p.wait("tensor", "eS", sil_mk[pi - 1])
                for ec in range(NEC):
                    lhsT = bslot[:, ec * 128:(ec + 1) * 128]
                    for th in range(2):
                        inst = t.matmul(
                            gA[:, th * 512:(th + 1) * 512], lhsT,
                            xn_fm[:, ec * 1024 + th * 512:
                                  ec * 1024 + (th + 1) * 512],
                            start=(ec == 0), stop=(ec == NEC - 1))
                gate_et = p.inc(inst, "eT", 1)
                if pi >= 2:
                    p.wait("tensor", "eV", h_ev[pi - 2])
                for ec in range(NEC):
                    lhsT = uslot[:, ec * 128:(ec + 1) * 128]
                    for th in range(2):
                        ups = (uB[:, th * 512:(th + 1) * 512] if pi % 2 == 0
                               else u45[th][:, :])
                        inst = t.matmul(
                            ups, lhsT,
                            xn_fm[:, ec * 1024 + th * 512:
                                  ec * 1024 + (th + 1) * 512],
                            start=(ec == 0), stop=(ec == NEC - 1))
                g_et[pi] = p.inc(inst, "eT", 1)
                p.wait("scalar", "eT", gate_et)
                if pi >= 1:
                    p.wait("scalar", "eV", h_ev[pi - 1])
                inst = s.activation(out=sgbuf[:, :], in_=gA[:, :], func=AF.Silu)
                sil_mk[pi] = p.inc(inst, "eS", 1)
                p.wait("vector", "eS", sil_mk[pi])
                p.wait("vector", "eT", g_et[pi])
                if pi >= 2:
                    p.wait("vector", "qS", h_wr[pi - 2])
                hb = hbuf[pi % 2]
                if pi % 2 == 0:
                    inst = v.tensor_tensor(out=hb[:, :], in0=sgbuf[:, :],
                                           in1=uB[:, :], op=OP.mult)
                else:
                    v.tensor_tensor(out=hb[:, 0:512], in0=sgbuf[:, 0:512],
                                    in1=u45[0][:, :], op=OP.mult)
                    inst = v.tensor_tensor(out=hb[:, 512:1024],
                                           in0=sgbuf[:, 512:1024],
                                           in1=u45[1][:, :], op=OP.mult)
                h_ev[pi] = p.inc(inst, "eV", 1)
                if pi >= 1:
                    p.wait("vector", "eT", sum_et[pi - 1])
                inst = v.tensor_tensor(out=hsqb[:, :], in0=hb[:, :],
                                       in1=hb[:, :], op=OP.mult)
                hq_ev[pi] = p.inc(inst, "eV", 1)
                p.wait("tensor", "eV", hq_ev[pi])
                for th in range(2):
                    inst = t.matmul(sq67[th][0:1, :], ones128[:, :],
                                    hsqb[:, th * 512:(th + 1) * 512],
                                    start=(pi == 0), stop=(pi == 63))
                sum_et[pi] = p.inc(inst, "eT", 1)
                p.wait("sync", "eV", h_ev[pi])
                inst = sy.dma_start(out=h_dram[128 * pi:128 * (pi + 1), :],
                                    in_=hb[:, :])
                h_wr[pi] = p.inc(inst, "qS", 16)
        gu_qS = p.val("qS")

        p.wait("vector", "eT", sum_et[63])
        for th in range(2):
            vsync(v.tensor_scalar(out=invr_row[0:1, th * 512:(th + 1) * 512],
                                  in0=sq67[th][0:1, :], scalar1=1.0 / H4,
                                  scalar2=RMS_EPS, op0=OP.mult, op1=OP.add))
            vsync(v.tensor_scalar(out=invr_row[0:1, th * 512:(th + 1) * 512],
                                  in0=invr_row[0:1, th * 512:(th + 1) * 512],
                                  scalar1=1e-12, scalar2=None, op0=OP.max))
        vdup(lambda: v.reciprocal(out=jnk[0:1, 0:1024], in_=invr_row[0:1, :]))
        p.wait("scalar", "eV", p.val("eV"))
        sdup(lambda: s.activation(out=invr_row[0:1, :], in_=jnk[0:1, 0:1024],
                                  func=AF.Sqrt))
        p.wait("vector", "eS", p.val("eS"))
        irv = vdup(lambda: v.tensor_copy(out=invr_bf[0:1, :],
                                         in_=invr_row[0:1, :]))
        p.wait("tensor", "eV", irv)
        for th in range(2):
            inst = t.matmul(prA[:, th * 512:(th + 1) * 512], ones1x[:, :],
                            invr_bf[0:1, th * 512:(th + 1) * 512],
                            start=True, stop=True)
        irt = p.inc(inst, "eT", 1)
        p.wait("vector", "eT", irt)
        inst = v.tensor_copy(out=invr_rep[:, :], in_=prA[:, :])
        ir_ev = p.inc(inst, "eV", 1)
        if DEBUG:
            p.wait("sync", "eV", ir_ev)
            p.inc(sy.dma_start(out=dbg["dbg_ir"][:, :], in_=invr_rep[:, :]), "qS", 16)
            p.wait("sync", "qS", gu_qS)
            p.inc(sy.dma_start(out=dbg["dbg_h"][:, :], in_=h_dram[:, :]), "qS", 16)

        p.wait("sync", "cY", 1)
        inst = sy.dma_start(out=y_all[:, :]
                            .rearrange("p2 (ec t2) -> p2 ec t2", ec=NEC),
                            in_=y_recv[:, :]
                            .rearrange("(ec p2) t2 -> p2 ec t2", p2=128))
        y_rl = p.inc(inst, "qS", 16)
        if DEBUG:
            p.wait("sync", "qS", y_rl)
            p.inc(sy.dma_start(out=dbg["dbg_y"][:, :], in_=y_all[:, :]), "qS", 16)
        pr_ld = [None] * 16
        pr_et = [None] * 16
        pr_ev = [None] * 16
        for fs in range(16):
            if fs < 4:
                p.wait("gpsimd", "eT", g_et[63])
            else:
                p.wait("gpsimd", "eT", pr_et[fs - 4])
            src = wprojT[:, 128 * fs:128 * (fs + 1)].rearrange(
                "(ec p2) f -> p2 ec f", p2=128)
            inst = gp.dma_start(
                out=wbuf[fs % 4][:, :].rearrange("p2 (ec f) -> p2 ec f", ec=16),
                in_=src)
            pr_ld[fs] = p.inc(inst, "qG", 16)
            ps = prA if fs % 2 == 0 else prB
            if fs >= 2:
                p.wait("tensor", "eV", pr_ev[fs - 2])
            if fs == 0:
                p.wait("tensor", "eV", ir_ev)
                p.wait("tensor", "qS", y_rl)
            p.wait("tensor", "qG", pr_ld[fs])
            for ec in range(NEC):
                lhsT = wbuf[fs % 4][:, ec * 128:(ec + 1) * 128]
                for th in range(2):
                    inst = t.matmul(
                        ps[:, th * 512:(th + 1) * 512], lhsT,
                        y_all[:, ec * 1024 + th * 512:ec * 1024 + (th + 1) * 512],
                        start=(ec == 0), stop=(ec == NEC - 1))
            pr_et[fs] = p.inc(inst, "eT", 1)
            p.wait("vector", "eT", pr_et[fs])
            inst = v.tensor_copy(out=xn_fm[:, fs * 1024:(fs + 1) * 1024],
                                 in_=ps[:, :])
            pr_ev[fs] = p.inc(inst, "eV", 1)
        s_eV = p.val("eV")
        if DEBUG:
            p.wait("sync", "eV", s_eV)
            p.inc(sy.dma_start(out=dbg["dbg_s"][:, :], in_=xn_fm[:, :]), "qS", 16)

        wp_ld = {}
        wp_et = {}
        h_ld = {}
        g_eS = {}
        for quad in range(4):
            p.wait("sync", f"cF{quad}", 1)
            for hc in range(64):
                e8 = hc // 8
                key = (quad, hc)
                if quad == 0 and hc == 0:
                    p.wait("sync", "eV", s_eV)
                    p.wait("sync", "qS", gu_qS)
                if hc >= 2 or quad > 0:
                    pk = (quad, hc - 2) if hc >= 2 else (quad - 1, 62 + hc)
                    p.wait("sync", "eT", wp_et[pk])
                inst = sy.dma_start(out=hbuf[hc % 2][:, :],
                                    in_=h_dram[128 * hc:128 * (hc + 1), :])
                h_ld[key] = p.inc(inst, "qS", 16)
                if hc % 8 == 0:
                    for f in range(4):
                        base = 3072 * e8 + 2048
                        slot = wbuf[f][:, (e8 % 2) * 1024:(e8 % 2) * 1024 + 1024]
                        inst = sy.dma_start(
                            out=slot.rearrange("p2 (hcs f2) -> p2 hcs f2", hcs=8),
                            in_=fw_dst[quad][base:base + 1024,
                                             128 * f:128 * (f + 1)]
                            .rearrange("(hcs p2) f2 -> p2 hcs f2", p2=128))
                        p.inc(inst, "qS", 16)
                    wp_ld[(quad, e8)] = p.val("qS")
                p.wait("tensor", "qS", h_ld[key])
                p.wait("tensor", "qS", wp_ld[(quad, e8)])
                if hc == 0:
                    if quad == 0:
                        p.wait("tensor", "eV", s_eV)
                    else:
                        for f in range(4):
                            p.wait("scalar", "eS", g_eS[(quad - 1, f)])
                        p.wait("tensor", "eS", g_eS[(quad - 1, 3)])
                for f in range(4):
                    lhsT = wbuf[f][:, (e8 % 2) * 1024 + (hc % 8) * 128:
                                   (e8 % 2) * 1024 + (hc % 8 + 1) * 128]
                    for th in range(2):
                        inst = t.matmul(
                            fp[f][:, th * 512:(th + 1) * 512], lhsT,
                            hbuf[hc % 2][:, th * 512:(th + 1) * 512],
                            start=(hc == 0), stop=(hc == 63))
                wp_et[key] = p.inc(inst, "eT", 1)
            for f in range(4):
                p.wait("vector", "eT", wp_et[(quad, 63)])
                p.wait("vector", "eV", ir_ev)
                if f >= 1:
                    p.wait("vector", "eS", g_eS[(quad, f - 1)])
                elif quad > 0:
                    p.wait("vector", "eS", g_eS[(quad - 1, 3)])
                inst = v.tensor_tensor(out=gt32[:, :], in0=fp[f][:, :],
                                       in1=invr_rep[:, :], op=OP.mult)
                gv = p.inc(inst, "eV", 1)
                p.wait("scalar", "eV", gv)
                inst = s.activation(
                    out=y_all[:, (4 * quad + f) * 1024:(4 * quad + f + 1) * 1024],
                    in_=gt32[:, :], func=AF.Silu)
                g_eS[(quad, f)] = p.inc(inst, "eS", 1)
        g_all_eS = p.val("eS")
        if DEBUG:
            p.wait("sync", "eS", g_all_eS)
            p.inc(sy.dma_start(out=dbg["dbg_g"][:, :], in_=y_all[:, :]), "qS", 16)

        x_rl = {}
        st_mk = {}
        tpw_ev = {}
        for tt in range(NTT):
            if tt >= 2:
                p.wait("sync", "qS", st_mk[tt - 2])
            inst = sy.dma_start(out=xf[tt % 2][:, :],
                                in_=x_in[128 * tt:128 * (tt + 1), :])
            x_rl[tt] = p.inc(inst, "qS", 16)
            wave_ev = []
            for wv in range(4):
                for k4 in range(4):
                    fc = 4 * wv + k4
                    if (tt, wv, k4) == (0, 0, 0):
                        p.wait("vector", "eS", g_all_eS)
                    p.wait("vector", "eT", p.val("eT"))
                    inst = v.tensor_tensor(
                        out=sgf[:, :],
                        in0=xn_fm[:, fc * 1024 + 128 * tt:
                                  fc * 1024 + 128 * (tt + 1)],
                        in1=y_all[:, fc * 1024 + 128 * tt:
                                  fc * 1024 + 128 * (tt + 1)],
                        op=OP.mult)
                    sgv = p.inc(inst, "eV", 1)
                    p.wait("tensor", "eV", sgv)
                    if k4 == 0:
                        wkey = (tt * 4 + wv) - 2
                        if wkey >= 0:
                            p.wait("tensor", "eV", tpw_ev[wkey])
                    inst = t.transpose(out=tp[wv % 2][:, k4 * 128:(k4 + 1) * 128],
                                       in_=sgf[:, :], identity=iden_f32[:, :])
                    p.inc(inst, "eT", 1)
                p.wait("vector", "eT", p.val("eT"))
                p.wait("vector", "qS", x_rl[tt])
                if tt >= 2 and wv == 0:
                    p.wait("vector", "qS", st_mk[tt - 2])
                inst = v.tensor_tensor(out=xf[2][:, 512 * wv:512 * (wv + 1)],
                                       in0=xf[tt % 2][:, 512 * wv:512 * (wv + 1)],
                                       in1=tp[wv % 2][:, :], op=OP.add)
                wave_ev.append(p.inc(inst, "eV", 1))
                tpw_ev[tt * 4 + wv] = wave_ev[-1]
            p.wait("sync", "eV", wave_ev[-1])
            inst = sy.dma_start(out=out_t[128 * tt:128 * (tt + 1), :],
                                in_=xf[2][:, :])
            st_mk[tt] = p.inc(inst, "qS", 16)
            if tt + 1 < NTT:
                p.wait("vector", "qS", st_mk[tt])
        p.wait("sync", "qS", p.val("qS"))

    with nc.Block() as blk:
        for name, starter in [("sync", blk.sync), ("gpsimd", blk.gpsimd),
                              ("scalar", blk.scalar), ("vector", blk.vector),
                              ("tensor", blk.tensor)]:
            def mk(nm):
                def f(eng):
                    emit(P(nc, sems, nm, eng))
                return f
            starter(mk(name))

    ctx.close()
    if sim:
        return nc.streams
    return nc


_CACHE = {}


def kernel(L0_attn_x, L0_attn_W_qkv, L0_attn_W_proj, L0_ffn_w_gate_up,
           L0_ffn_w_proj):
    x2d = np.ascontiguousarray(np.asarray(L0_attn_x, dtype=np.float32)
                               .reshape(T, D))
    wqkvT = np.ascontiguousarray(np.asarray(L0_attn_W_qkv, np.float32).T)
    wprojT = np.ascontiguousarray(np.asarray(L0_attn_W_proj, np.float32).T)
    guT = np.asarray(L0_ffn_w_gate_up, np.float32).T
    wpT = np.asarray(L0_ffn_w_proj, np.float32).T

    dmask_h = _diag_masks()
    iden_bf_h = np.eye(128, dtype=np.float32).astype(BF)
    iden_f32_h = np.eye(128, dtype=np.float32)
    gmat_h = np.zeros((128, 2), np.float32)
    gmat_h[0:64, 0] = 1.0
    gmat_h[64:128, 1] = 1.0
    gmat_h = gmat_h.astype(BF)
    gmat2_h = np.zeros((2, 128), np.float32)
    gmat2_h[0, 0:64] = 2.0
    gmat2_h[1, 64:128] = 2.0
    gmat2_h = gmat2_h.astype(BF)
    ones128_h = np.ones((128, 1), np.float32).astype(BF)
    ones1x_h = np.ones((1, 128), np.float32).astype(BF)

    if "nc" not in _CACHE:
        _CACHE["nc"] = build_program()
    nc = _CACHE["nc"]

    in_maps = []
    for i in range(8):
        pos = np.arange(TOWN * i, TOWN * (i + 1))
        C, S = _rope_tables(pos)
        in_maps.append({
            "x": np.ascontiguousarray(x2d[TOWN * i:TOWN * (i + 1)]),
            "wqkvT": wqkvT,
            "wprojT": wprojT,
            "gu_shard": np.ascontiguousarray(guT[:, 2048 * i:2048 * (i + 1)]),
            "wp_shard": np.ascontiguousarray(wpT[1024 * i:1024 * (i + 1), :]),
            "cosk": C, "sink": S,
            "dmask": dmask_h, "iden_bf": iden_bf_h, "iden_f32": iden_f32_h,
            "gmat": gmat_h, "gmat2": gmat2_h, "ones128c": ones128_h,
            "ones1x": ones1x_h,
        })

    res = run_bass_kernel_spmd(nc, in_maps, core_ids=list(range(8)))
    _CACHE["res"] = res
    out = np.empty((T, D), dtype=np.float32)
    for i in range(8):
        out[TOWN * i:TOWN * (i + 1), :] = res.results[i]["out"]
    return out.reshape(1, T, D)


# version 10
# speedup vs baseline: 1.2435x; 1.2435x over previous
import math
import contextlib
import collections
import numpy as np
import ml_dtypes

import concourse.bass as bass
import concourse.mybir as mybir
from concourse.bass_utils import run_bass_kernel_spmd
from concourse.aot_env import allow_ici_all_to_all

F32 = mybir.dt.float32
BF16 = mybir.dt.bfloat16
AF = mybir.ActivationFunctionType
OP = mybir.AluOpType
AX = mybir.AxisListType
BF = ml_dtypes.bfloat16

T = 8192
D = 2048
NH = 16
HD = 128
H4 = 8192
TOWN = 1024
NTT = 8
NEC = 16
ISQ = 1.0 / math.sqrt(128.0)
RMS_EPS = 1.1920929e-07


class _FakeInst:
    def then_inc(self, *a, **k):
        return self


class _FakeEng:
    def __getattr__(self, name):
        def f(*a, **k):
            return _FakeInst()
        return f


_FAKE = _FakeEng()


class P:

    def __init__(self, nc, sems, which, engine):
        self.nc = nc
        self.sems = sems
        self.which = which
        self.engine = engine
        self.cnt = collections.defaultdict(int)

    def eng(self, name):
        return self.engine if name == self.which else _FAKE

    def inc(self, inst, sem, n=1):
        self.cnt[sem] += n
        if inst is not None and not isinstance(inst, _FakeInst):
            inst.then_inc(self.sems[sem], n)
        return self.cnt[sem]

    def val(self, sem):
        return self.cnt[sem]

    def wait(self, engname, sem, val=None):
        if engname == self.which:
            v = self.cnt[sem] if val is None else val
            if v > 0:
                self.engine.wait_ge(self.sems[sem], v)


def _rope_tables(positions):
    inv_freq = 1.0 / (10000.0 ** (np.arange(0, HD, 2, dtype=np.float64) / HD))
    ang = positions[:, None].astype(np.float64) * inv_freq[None, :]
    cos = np.cos(ang).T
    sin = np.sin(ang).T
    C = np.empty((HD, len(positions)), dtype=np.float32)
    S = np.empty((HD, len(positions)), dtype=np.float32)
    C[0::2, :] = cos
    C[1::2, :] = cos
    S[0::2, :] = -sin
    S[1::2, :] = sin
    return C, S


def _diag_masks():
    m = np.zeros((128, 4 * 512), dtype=np.float32)
    for blk in range(4):
        for kt in range(128):
            q0 = 128 * blk + kt
            if q0 < 512:
                m[kt, blk * 512 + q0:(blk + 1) * 512] = 1.0
    return m.astype(BF)


class _FakeAP:
    shape = (128, 1024)

    def __getitem__(self, k):
        return self

    def rearrange(self, *a, **kw):
        return self

    def ap(self):
        return self

    def opt(self):
        return self


class _FakeCtx:
    def __init__(self, val):
        self.val = val

    def __enter__(self):
        return self.val

    def __exit__(self, *a):
        return False


class _RecInst:
    def __init__(self, stream, idx):
        self.stream = stream
        self.idx = idx

    def then_inc(self, sem, n=1):
        self.stream[self.idx]["incs"].append((sem, n))
        return self


class _RecEngine:
    def __init__(self, streams, name):
        self._streams = streams
        self._name = name
        self._pw = []

    def wait_ge(self, sem, val):
        self._pw.append((sem, val))

    def __getattr__(self, opname):
        if opname.startswith("_"):
            raise AttributeError(opname)

        def f(*a, **kw):
            st = self._streams[self._name]
            st.append({"op": opname, "waits": list(self._pw), "incs": []})
            self._pw = []
            return _RecInst(st, len(st) - 1)
        return f


class _FakeBlock:
    def __init__(self, streams):
        self._streams = streams

    def __enter__(self):
        return self

    def __exit__(self, *a):
        return False

    def __getattr__(self, name):
        if name.startswith("_"):
            raise AttributeError(name)

        def starter(f):
            f(_RecEngine(self._streams, name))
        return starter


class _FakeNC:
    def __init__(self):
        self.streams = {n: [] for n in
                        ["sync", "gpsimd", "scalar", "vector", "tensor"]}

    def dram_tensor(self, *a, **k):
        return _FakeAP()

    def sbuf_tensor(self, *a, **k):
        return _FakeCtx(_FakeAP())

    def psum_tensor(self, *a, **k):
        return _FakeCtx(_FakeAP())

    def semaphore(self, name=None):
        return _FakeCtx(name)

    def Block(self, *a, **k):
        return _FakeBlock(self.streams)


DEBUG = False


def build_program(sim=False):
    nc = _FakeNC() if sim else bass.Bass("TRN2", target_bir_lowering=False)

    x_in = nc.dram_tensor("x", [TOWN, D], F32, kind="ExternalInput")
    wqkvT = nc.dram_tensor("wqkvT", [D, 3 * D], F32, kind="ExternalInput")
    wprojT = nc.dram_tensor("wprojT", [D, D], F32, kind="ExternalInput")
    gu_shard = nc.dram_tensor("gu_shard", [D, D], F32, kind="ExternalInput")
    wp_shard = nc.dram_tensor("wp_shard", [1024, D], F32, kind="ExternalInput")
    cosk_in = nc.dram_tensor("cosk", [HD, TOWN], F32, kind="ExternalInput")
    sink_in = nc.dram_tensor("sink", [HD, TOWN], F32, kind="ExternalInput")
    dmask_in = nc.dram_tensor("dmask", [128, 4 * 512], BF16, kind="ExternalInput")
    iden_bf_in = nc.dram_tensor("iden_bf", [128, 128], BF16, kind="ExternalInput")
    iden_f32_in = nc.dram_tensor("iden_f32", [128, 128], F32, kind="ExternalInput")
    gmat_in = nc.dram_tensor("gmat", [128, 2], BF16, kind="ExternalInput")
    gmat2_in = nc.dram_tensor("gmat2", [2, 128], BF16, kind="ExternalInput")
    ones128_in = nc.dram_tensor("ones128c", [128, 1], BF16, kind="ExternalInput")
    ones1x_in = nc.dram_tensor("ones1x", [1, 128], BF16, kind="ExternalInput")

    out_t = nc.dram_tensor("out", [TOWN, D], F32, kind="ExternalOutput")
    dbg = {}
    if DEBUG:
        for nm, shp, dt_ in [("dbg_xn", [128, NEC * TOWN], BF16),
                             ("dbg_y", [128, NH * TOWN], BF16),
                             ("dbg_s", [128, NEC * TOWN], BF16),
                             ("dbg_g", [128, NEC * TOWN], BF16),
                             ("dbg_nb", [128, 2], F32),
                             ("dbg_ir", [128, TOWN], BF16),
                             ("dbg_h", [H4, TOWN], BF16),
                             ("dbg_kr", [8 * 258, TOWN], BF16),
                             ("dbg_vr", [8 * TOWN, 256], BF16),
                             ("dbg_qr", [8 * 258, TOWN], BF16)]:
            dbg[nm] = nc.dram_tensor(nm, shp, dt_, kind="ExternalOutput")
        for nm, shp, dt_ in [("dbg_fw0", [8 * 3072, 512], BF16),
                             ("dbg_p0", [128, 1024], BF16),
                             ("dbg_y0", [128, 512], BF16),
                             ("dbg_yu", [128, 512], BF16),
                             ("dbg_rs", [1, 512], BF16)]:
            dbg[nm] = nc.dram_tensor(nm, shp, dt_, kind="ExternalOutput")

    k_src = nc.dram_tensor("k_src", [8 * 258, TOWN], BF16)
    q_src = nc.dram_tensor("q_src", [8 * 258, TOWN], BF16)
    v_src = nc.dram_tensor("v_src", [8 * TOWN, 256], BF16)
    y_src = nc.dram_tensor("y_src", [8 * 256, TOWN], BF16)
    k_recv = nc.dram_tensor("k_recv", [8 * 258, TOWN], BF16)
    q_recv = nc.dram_tensor("q_recv", [8 * 258, TOWN], BF16)
    v_recv = nc.dram_tensor("v_recv", [8 * TOWN, 256], BF16)
    y_recv = nc.dram_tensor("y_recv", [8 * 256, TOWN], BF16)
    fw_src = [nc.dram_tensor(f"fw_src{p}", [3072, 512], BF16) for p in range(4)]
    fw_dst = [nc.dram_tensor(f"fw_dst{p}", [8 * 3072, 512], BF16) for p in range(4)]
    h_dram = nc.dram_tensor("h_dram", [H4, TOWN], BF16)

    ctx = contextlib.ExitStack()
    E = ctx.enter_context

    sem_names = (["qS", "qG", "qA", "eT", "eV", "eS",
                  "cK", "cV", "cQ", "cY"] + [f"cF{w}" for w in range(4)])
    sems = {n: E(nc.semaphore(n)) for n in sem_names}

    xn_fm = E(nc.sbuf_tensor("xn_fm", [128, NEC * TOWN], BF16))
    y_all = E(nc.sbuf_tensor("y_all", [128, NH * TOWN], BF16))
    wbuf = [E(nc.sbuf_tensor(f"wbuf{j}", [128, 2048], BF16)) for j in range(4)]
    xf = [E(nc.sbuf_tensor(f"xf{j}", [128, D], F32)) for j in range(3)]
    xnt = E(nc.sbuf_tensor("xnt", [128, D], BF16))
    ropeC = E(nc.sbuf_tensor("ropeC", [128, TOWN], F32))
    ropeS = E(nc.sbuf_tensor("ropeS", [128, TOWN], F32))
    dmask = E(nc.sbuf_tensor("dmask_sb", [128, 4 * 512], BF16))
    krt = E(nc.sbuf_tensor("krt", [128, TOWN], BF16))
    vfm = E(nc.sbuf_tensor("vfm", [128, TOWN], BF16))
    vown = E(nc.sbuf_tensor("vown", [128, TOWN], BF16))
    pful = [E(nc.sbuf_tensor(f"pful{j}", [128, 1024], BF16)) for j in range(2)]
    sq_tmp = E(nc.sbuf_tensor("sq_tmp", [128, TOWN], BF16))
    sgbuf = E(nc.sbuf_tensor("sgbuf", [128, TOWN], BF16))
    hbuf = [E(nc.sbuf_tensor(f"hbuf{j}", [128, TOWN], BF16)) for j in range(4)]
    hsqb = E(nc.sbuf_tensor("hsqb", [128, TOWN], BF16))
    invr_rep = E(nc.sbuf_tensor("invr_rep", [128, TOWN], BF16))
    gt32 = E(nc.sbuf_tensor("gt32", [128, TOWN], F32))
    ysb_tmp = E(nc.sbuf_tensor("ysb_tmp", [128, 512], BF16))
    iden_bf = E(nc.sbuf_tensor("iden_bf_sb", [128, 128], BF16))
    iden_f32 = E(nc.sbuf_tensor("iden_f32_sb", [128, 128], F32))
    gmat = E(nc.sbuf_tensor("gmat_sb", [128, 2], BF16))
    gmat2 = E(nc.sbuf_tensor("gmat2_sb", [2, 128], BF16))
    ones128 = E(nc.sbuf_tensor("ones128_sb", [128, 1], BF16))
    ones1x = E(nc.sbuf_tensor("ones1x_sb", [1, 128], BF16))
    negB = E(nc.sbuf_tensor("negB", [128, 2], F32))
    stat = E(nc.sbuf_tensor("stat", [128, 8], F32))
    knrow = E(nc.sbuf_tensor("knrow", [1, 4096], BF16))
    kn_stage = E(nc.sbuf_tensor("kn_stage", [1, TOWN], BF16))
    invr_row = E(nc.sbuf_tensor("invr_row", [1, TOWN], F32))
    invr_bf = E(nc.sbuf_tensor("invr_bf", [1, TOWN], BF16))
    rsum_sb = E(nc.sbuf_tensor("rsum_sb", [1, 512], F32))
    rs_bf = E(nc.sbuf_tensor("rs_bf", [1, 512], BF16))
    km_sb = E(nc.sbuf_tensor("km_sb", [1, 8], F32))
    bp_sb = E(nc.sbuf_tensor("bp_sb", [1, 8], F32))
    bn_bf = E(nc.sbuf_tensor("bn_bf", [1, 8], BF16))
    qthr = E(nc.sbuf_tensor("qthr", [2, 512], BF16))
    sgf = E(nc.sbuf_tensor("sgf", [128, 128], F32))
    jnk = E(nc.sbuf_tensor("jnk", [128, 2048], F32))

    with contextlib.ExitStack() as pA:
        psA = pA.enter_context(nc.psum_tensor("psA", [128, 1024], F32))
        psB = pA.enter_context(nc.psum_tensor("psB", [128, 1024], F32))
        vt_ps = pA.enter_context(nc.psum_tensor("vt_ps", [128, 1024], BF16))
        kn_ps = pA.enter_context(nc.psum_tensor("kn_ps", [1, 1024], F32))
        bB_ps = pA.enter_context(nc.psum_tensor("bB_ps", [128, 2], F32))
    with contextlib.ExitStack() as pB:
        scps = [pB.enter_context(nc.psum_tensor(f"sc{j}", [128, 1024], F32))
                for j in range(2)]
        yT = [pB.enter_context(nc.psum_tensor(f"yT{j}", [128, 512], F32))
              for j in range(2)]
        smp = [pB.enter_context(nc.psum_tensor(f"smp{j}", [128, 512], F32))
               for j in range(2)]
    with contextlib.ExitStack() as pD:
        gA = pD.enter_context(nc.psum_tensor("gA", [128, 1024], F32))
        uB = pD.enter_context(nc.psum_tensor("uB", [128, 1024], F32))
        u45 = [pD.enter_context(nc.psum_tensor(f"u4{j}", [128, 512], F32))
               for j in range(2)]
        sq67 = [pD.enter_context(nc.psum_tensor(f"sq6{j}", [1, 512], F32))
                for j in range(2)]
    with contextlib.ExitStack() as pC:
        prA = pC.enter_context(nc.psum_tensor("prA", [128, 1024], F32))
        prB = pC.enter_context(nc.psum_tensor("prB", [128, 1024], F32))
    with contextlib.ExitStack() as pE:
        fp = [pE.enter_context(nc.psum_tensor(f"fp{j}", [128, 1024], F32))
              for j in range(4)]
    with contextlib.ExitStack() as pF:
        tp = [pF.enter_context(nc.psum_tensor(f"tp{j}", [128, 512], F32))
              for j in range(2)]

    fsubs = ([("K", h) for h in range(NH)] + [("Q", h) for h in range(NH)]
             + [("V", h) for h in range(NH)])
    NSUB = len(fsubs)

    def fcol(kind, h):
        return {"Q": 0, "K": D, "V": 2 * D}[kind] + 128 * h

    swap_mask = []
    for jj in range(16):
        swap_mask += [2 * jj + 1, 2 * jj]

    qtiles = []
    for w in range(4):
        for ch in range(16):
            qtiles.append((0, ch, w))
        for ch in range(8):
            qtiles.append((1, ch, w))

    def emit(p: P):
        t = p.eng("tensor")
        v = p.eng("vector")
        s = p.eng("scalar")
        sy = p.eng("sync")
        gp = p.eng("gpsimd")

        def vsync(inst):
            m = p.inc(inst, "eV", 1)
            p.wait("vector", "eV", m)
            return m

        def vdup(emit):
            emit()
            return p.inc(emit(), "eV", 1)

        def sdup(emit):
            emit()
            return p.inc(emit(), "eS", 1)

        for src, dst in [
            (dmask_in, dmask), (iden_bf_in, iden_bf), (iden_f32_in, iden_f32),
            (gmat_in, gmat), (gmat2_in, gmat2), (ones128_in, ones128),
            (ones1x_in, ones1x), (cosk_in, ropeC), (sink_in, ropeS),
        ]:
            p.inc(sy.dma_start(out=dst[:, :], in_=src[:, :]), "qS", 16)
        const_ld = p.val("qS")

        ln_ev = []
        ln_evac = []
        for tt in range(NTT):
            b = tt % 2
            if tt >= 2:
                p.wait("sync", "eV", ln_ev[tt - 2])
            p.inc(sy.dma_start(out=xf[b][:, :], in_=x_in[128 * tt:128 * (tt + 1), :]),
                  "qS", 16)
            xld = p.val("qS")
            p.wait("vector", "qS", xld)
            v.tensor_tensor(out=xf[2][:, :], in0=xf[b][:, :], in1=xf[b][:, :],
                            op=OP.mult)
            vsync(v.tensor_reduce(out=stat[:, 0:1], in_=xf[b][:, :], axis=AX.X,
                                  op=OP.add))
            vsync(v.tensor_reduce(out=stat[:, 1:2], in_=xf[2][:, :], axis=AX.X,
                                  op=OP.add))
            vsync(v.tensor_scalar(out=stat[:, 2:3], in0=stat[:, 0:1],
                                  scalar1=1.0 / D, scalar2=None, op0=OP.mult))
            vsync(v.tensor_scalar(out=stat[:, 3:4], in0=stat[:, 1:2],
                                  scalar1=1.0 / D, scalar2=None, op0=OP.mult))
            vsync(v.tensor_tensor(out=stat[:, 4:5], in0=stat[:, 2:3],
                                  in1=stat[:, 2:3], op=OP.mult))
            vsync(v.tensor_tensor(out=stat[:, 5:6], in0=stat[:, 3:4],
                                  in1=stat[:, 4:5], op=OP.subtract))
            vsync(v.tensor_scalar(out=stat[:, 6:7], in0=stat[:, 5:6],
                                  scalar1=1e-5, scalar2=1e-6,
                                  op0=OP.add, op1=OP.max))
            vdup(lambda: v.reciprocal(out=stat[:, 7:8], in_=stat[:, 6:7]))
            p.wait("scalar", "eV", p.val("eV"))
            sdup(lambda: s.activation(out=stat[:, 6:7], in_=stat[:, 7:8],
                                      func=AF.Sqrt))
            p.wait("vector", "eS", p.val("eS"))
            inst = v.tensor_scalar(out=xnt[:, :], in0=xf[b][:, :],
                                   scalar1=stat[:, 2:3], scalar2=stat[:, 6:7],
                                   op0=OP.subtract, op1=OP.mult)
            ln_ev.append(p.inc(inst, "eV", 1))
            p.wait("tensor", "eV", p.val("eV"))
            if tt == 0:
                p.wait("tensor", "qS", const_ld)
            for half in range(2):
                if ln_evac:
                    p.wait("tensor", "eV", ln_evac[-1])
                for j in range(8):
                    ec = half * 8 + j
                    inst = t.transpose(out=vt_ps[:, j * 128:(j + 1) * 128],
                                       in_=xnt[:, ec * 128:(ec + 1) * 128],
                                       identity=iden_bf[:, :])
                p.inc(inst, "eT", 1)
                p.wait("vector", "eT", p.val("eT"))
                dst = xn_fm[:, :].rearrange("p (ec tt2 c) -> p ec tt2 c",
                                            ec=NEC, tt2=NTT)
                inst = v.tensor_copy(
                    out=dst[:, half * 8:(half + 1) * 8, tt, :],
                    in_=vt_ps[:, :].rearrange("p (j c) -> p j c", j=8))
                ln_evac.append(p.inc(inst, "eV", 1))
        ln_all = p.val("eV")
        if DEBUG:
            p.wait("sync", "eV", ln_all)
            p.inc(sy.dma_start(out=dbg["dbg_xn"][:, :], in_=xn_fm[:, :]), "qS", 16)

        wq_ld = [None] * NSUB
        wq_cast = [None] * NSUB
        et_sub = [None] * NSUB
        ev_sub = [None] * NSUB
        stg_wr = {}
        qn_ev = {}

        for fs, (kind, h) in enumerate(fsubs):
            pair, l = divmod(h, 2)
            fb = fcol(kind, h)
            if fs >= 2:
                p.wait("sync", "eS", wq_cast[fs - 2])
            else:
                p.wait("sync", "eV", ln_ev[6 + fs])
            src = wqkvT[:, fb:fb + 128].rearrange("(ec p2) f -> p2 ec f", p2=128)
            inst = sy.dma_start(
                out=xf[fs % 2][:, :].rearrange("p2 (ec f) -> p2 ec f", ec=16),
                in_=src)
            wq_ld[fs] = p.inc(inst, "qS", 16)
            p.wait("scalar", "qS", wq_ld[fs])
            if fs >= 4:
                p.wait("scalar", "eT", et_sub[fs - 4])
            inst = s.activation(out=wbuf[fs % 4][:, :], in_=xf[fs % 2][:, :],
                                func=AF.Copy)
            wq_cast[fs] = p.inc(inst, "eS", 1)
            ps = psA if fs % 2 == 0 else psB
            if fs >= 2:
                p.wait("tensor", "eV", ev_sub[fs - 2])
            if fs == 0:
                p.wait("tensor", "eV", ln_all)
            p.wait("tensor", "eS", wq_cast[fs])
            for ec in range(NEC):
                lhsT = wbuf[fs % 4][:, ec * 128:(ec + 1) * 128]
                for th in range(2):
                    inst = t.matmul(
                        ps[:, th * 512:(th + 1) * 512], lhsT,
                        xn_fm[:, ec * 1024 + th * 512:ec * 1024 + (th + 1) * 512],
                        start=(ec == 0), stop=(ec == NEC - 1))
            et_sub[fs] = p.inc(inst, "eT", 1)
            p.wait("vector", "eT", et_sub[fs])
            if kind in ("K", "Q"):
                if fs >= 1 and (fs - 1) in stg_wr:
                    p.wait("vector", "qG", stg_wr[fs - 1])
                v.stream_shuffle(out=xf[2][:, 0:1024], in_=ps[:, :], mask=swap_mask)
                v.tensor_tensor(out=gt32[:, :], in0=ps[:, :], in1=ropeC[:, :],
                                op=OP.mult)
                v.tensor_tensor(out=xf[2][:, 1024:2048], in0=xf[2][:, 0:1024],
                                in1=ropeS[:, :], op=OP.mult)
                v.tensor_tensor(out=krt[:, :], in0=gt32[:, :],
                                in1=xf[2][:, 1024:2048], op=OP.add)
                inst = v.tensor_tensor(out=krt[:, :], in0=gt32[:, :],
                                       in1=xf[2][:, 1024:2048], op=OP.add)
                ev_sub[fs] = p.inc(inst, "eV", 1)
                p.wait("scalar", "eV", ev_sub[fs])
                s.activation(out=sq_tmp[:, :], in_=krt[:, :], func=AF.Square)
                inst = s.activation(out=sq_tmp[:, :], in_=krt[:, :], func=AF.Square)
                sqm = p.inc(inst, "eS", 1)
                p.wait("tensor", "eS", sqm)
                p.wait("tensor", "eV", p.val("eV"))
                for th in range(2):
                    inst = t.matmul(kn_ps[0:1, th * 512:(th + 1) * 512],
                                    ones128[:, :],
                                    sq_tmp[:, th * 512:(th + 1) * 512],
                                    start=True, stop=True)
                knm = p.inc(inst, "eT", 1)
                p.wait("vector", "eT", knm)
                knc = vdup(lambda: v.tensor_copy(out=kn_stage[0:1, :],
                                                 in_=kn_ps[0:1, :]))
                dstt = k_src if kind == "K" else q_src
                p.wait("gpsimd", "eV", ev_sub[fs])
                inst = gp.dma_start(
                    out=dstt[258 * pair + 128 * l:258 * pair + 128 * (l + 1), :],
                    in_=krt[:, :])
                p.inc(inst, "qG", 16)
                p.wait("gpsimd", "eV", knc)
                inst = gp.dma_start(
                    out=dstt[258 * pair + 256 + l:258 * pair + 257 + l, :],
                    in_=kn_stage[0:1, :])
                stg_wr[fs] = p.inc(inst, "qG", 16)
            else:
                v.tensor_copy(out=vfm[:, :], in_=ps[:, :])
                inst = v.tensor_copy(out=vfm[:, :], in_=ps[:, :])
                vc = p.inc(inst, "eV", 1)
                ev_sub[fs] = vc
                p.wait("tensor", "eV", vc)
                p.wait("tensor", "eV", p.val("eV"))
                for tb in range(8):
                    inst = t.transpose(out=vt_ps[:, tb * 128:(tb + 1) * 128],
                                       in_=vfm[:, tb * 128:(tb + 1) * 128],
                                       identity=iden_bf[:, :])
                vt = p.inc(inst, "eT", 1)
                p.wait("vector", "eT", vt)
                if fs >= 1 and (fs - 1) in stg_wr:
                    p.wait("vector", "qG", stg_wr[fs - 1])
                v.tensor_copy(out=vown[:, :], in_=vt_ps[:, :])
                inst = v.tensor_copy(out=vown[:, :], in_=vt_ps[:, :])
                voc = p.inc(inst, "eV", 1)
                p.wait("gpsimd", "eV", voc)
                inst = gp.dma_start(
                    out=v_src[1024 * pair:1024 * (pair + 1),
                              128 * l:128 * (l + 1)]
                    .rearrange("(tb p2) d -> p2 tb d", p2=128),
                    in_=vown[:, :].rearrange("p2 (tb d) -> p2 tb d", tb=8))
                stg_wr[fs] = p.inc(inst, "qG", 16)
            if fs in (NH - 1, 2 * NH - 1, 3 * NH - 1):
                p.wait("gpsimd", "qG", stg_wr[fs])
                kindmap = {NH - 1: ("cK", k_src, k_recv),
                           2 * NH - 1: ("cQ", q_src, q_recv),
                           3 * NH - 1: ("cV", v_src, v_recv)}
                cs, srcten, dstten = kindmap[fs]
                with allow_ici_all_to_all():
                    inst = gp.collective_compute(
                        "AllToAll", OP.bypass, replica_groups=[list(range(8))],
                        ins=[srcten.ap().opt()], outs=[dstten.ap().opt()])
                p.inc(inst, cs, 1)

        qkv_eT = p.val("eT")
        qkv_eV = p.val("eV")

        knrow_red = 0
        for l in range(2):
            for which, csem, ten, cell0 in [(0, "cK", k_recv, 0),
                                            (1, "cQ", q_recv, 4)]:
                p.wait("scalar", csem, 1)
                for rnd in range(2):
                    if knrow_red:
                        p.wait("scalar", "eV", knrow_red)
                    for rr in range(4):
                        r = 4 * rnd + rr
                        inst = s.dma_start(
                            out=knrow[0:1, rr * 1024:(rr + 1) * 1024],
                            in_=ten[258 * r + 256 + l:258 * r + 257 + l, :])
                        p.inc(inst, "qA", 16)
                    p.wait("vector", "qA", p.val("qA"))
                    cell = km_sb[0:1, cell0 + 2 * l + rnd:cell0 + 2 * l + rnd + 1]
                    knrow_red = vsync(v.tensor_reduce(out=cell, in_=knrow[0:1, :],
                                                      axis=AX.X, op=OP.max))
                inst = v.tensor_tensor(
                    out=km_sb[0:1, cell0 + 2 * l:cell0 + 2 * l + 1],
                    in0=km_sb[0:1, cell0 + 2 * l:cell0 + 2 * l + 1],
                    in1=km_sb[0:1, cell0 + 2 * l + 1:cell0 + 2 * l + 2], op=OP.max)
                vsync(inst)
        for l in range(2):
            inst = v.tensor_tensor(out=bp_sb[0:1, l:l + 1],
                                   in0=km_sb[0:1, 2 * l:2 * l + 1],
                                   in1=km_sb[0:1, 4 + 2 * l:5 + 2 * l],
                                   op=OP.mult)
            vsync(inst)
        vdup(lambda: v.tensor_copy(out=bp_sb[0:1, 4:6], in_=bp_sb[0:1, 0:2]))
        p.wait("scalar", "eV", p.val("eV"))
        sdup(lambda: s.activation(out=bp_sb[0:1, 2:4], in_=bp_sb[0:1, 4:6],
                                  func=AF.Sqrt))
        p.wait("vector", "eS", p.val("eS"))
        vdup(lambda: v.tensor_scalar(out=bn_bf[0:1, 0:2], in0=bp_sb[0:1, 2:4],
                                     scalar1=-ISQ, scalar2=-60.0,
                                     op0=OP.mult, op1=OP.max))
        bnv = p.val("eV")
        p.wait("tensor", "eV", bnv)
        for l in range(2):
            inst = t.matmul(bB_ps[:, l:l + 1], ones1x[:, :],
                            bn_bf[0:1, l:l + 1], start=True, stop=True)
        bmm = p.inc(inst, "eT", 1)
        p.wait("vector", "eT", bmm)
        negB_ev = vdup(lambda: v.tensor_copy(out=negB[:, :], in_=bB_ps[:, :]))
        if DEBUG:
            p.wait("sync", "eV", negB_ev)
            p.inc(sy.dma_start(out=dbg["dbg_nb"][:, :], in_=negB[:, :]), "qS", 16)
            p.wait("sync", "cK", 1)
            p.inc(sy.dma_start(out=dbg["dbg_kr"][:, :], in_=k_recv[:, :]), "qS", 16)
            p.wait("sync", "cQ", 1)
            p.inc(sy.dma_start(out=dbg["dbg_qr"][:, :], in_=q_recv[:, :]), "qS", 16)
            p.wait("sync", "cV", 1)
            p.inc(sy.dma_start(out=dbg["dbg_vr"][:, :], in_=v_recv[:, :]), "qS", 16)

        fw_wr = [0] * 4
        q_hv = []
        prev_gs = 0
        for ti, (which, ch, w) in enumerate(qtiles):
            buf = wbuf[ti % 2]
            hat = wbuf[2 + ti % 2]
            srcT = gu_shard if which == 0 else wp_shard
            if ti < 2:
                p.wait("gpsimd", "eT", qkv_eT)
            else:
                p.wait("gpsimd", "eV", q_hv[ti - 2])
            inst = gp.dma_start(out=buf[:, 0:512],
                                in_=srcT[128 * ch:128 * (ch + 1),
                                         512 * w:512 * (w + 1)])
            ldv = p.inc(inst, "qG", 16)
            p.wait("scalar", "qG", ldv)
            if ti >= 1:
                p.wait("scalar", "eV", q_hv[ti - 1])
                p.wait("scalar", "eT", prev_gs)
            inst = s.activation(out=sq_tmp[:, 0:512], in_=buf[:, 0:512],
                                func=AF.Abs)
            absv = p.inc(inst, "eS", 1)
            inst = s.activation(out=sgbuf[:, 0:512], in_=buf[:, 0:512],
                                func=AF.Sign)
            sg = p.inc(inst, "eS", 1)
            p.wait("tensor", "eS", absv)
            if ti == 0:
                p.wait("tensor", "eV", qkv_eV)
            inst = t.matmul(psA[0:2, 0:512], gmat[:, :], sq_tmp[:, 0:512],
                            start=True, stop=True)
            gs = p.inc(inst, "eT", 1)
            prev_gs = gs
            p.wait("vector", "eT", gs)
            thv = vdup(lambda: v.tensor_scalar(out=qthr[:, :],
                                               in0=psA[0:2, 0:512],
                                               scalar1=0.5 / 64.0, scalar2=5e-9,
                                               op0=OP.mult, op1=OP.max))
            p.wait("tensor", "eV", thv)
            inst = t.matmul(psB[:, 0:512], gmat2[:, :], qthr[:, :],
                            start=True, stop=True)
            tr = p.inc(inst, "eT", 1)
            p.wait("vector", "eT", tr)
            p.wait("vector", "eS", sg)
            v.tensor_scalar(out=hsqb[:, 0:512], in0=sq_tmp[:, 0:512],
                            scalar1=2.0, scalar2=None, op0=OP.mult)
            v.tensor_tensor(out=hbuf[0][:, 0:512], in0=hsqb[:, 0:512],
                            in1=psB[:, 0:512], op=OP.is_gt)
            v.tensor_tensor(out=hbuf[1][:, 0:512], in0=hbuf[0][:, 0:512],
                            in1=sgbuf[:, 0:512], op=OP.mult)
            inst = v.tensor_tensor(out=hat[:, 0:512], in0=hbuf[1][:, 0:512],
                                   in1=psB[:, 0:512], op=OP.mult)
            hv = p.inc(inst, "eV", 1)
            q_hv.append(hv)
            p.wait("gpsimd", "eV", hv)
            row0 = 128 * ch if which == 0 else 2048 + 128 * ch
            inst = gp.dma_start(out=fw_src[w][row0:row0 + 128, :],
                                in_=hat[:, 0:512])
            fw_wr[w] = p.inc(inst, "qG", 16)
        quant_eV = p.val("eV")
        quant_qG = p.val("qG")
        for w in range(4):
            p.wait("gpsimd", "qG", fw_wr[w])
            inst = gp.collective_compute(
                "AllGather", OP.bypass, replica_groups=[list(range(8))],
                ins=[fw_src[w].ap().opt()], outs=[fw_dst[w].ap().opt()])
            p.inc(inst, f"cF{w}", 1)

        if DEBUG:
            p.wait("sync", "cF0", 1)
            p.inc(sy.dma_start(out=dbg["dbg_fw0"][:, :], in_=fw_dst[0][:, :]),
                  "qS", 16)

        exp_mk = {}
        pv_mk = {}
        ep_sc = {}
        ep_ev = {}
        KVSLOT = [0, 1, 3]
        gc_marks = []
        q_marks = {}
        yst_wr = {}
        gc = 0
        last_ep_ev = None
        for l in range(2):
            for pp in range(8):
                nbA = 4 * (2 * pp + 1)
                nbB = nbA + 4
                nch = nbB // 4
                if (l, pp) == (0, 0):
                    p.wait("sync", "cQ", 1)
                    p.wait("sync", "cK", 1)
                    p.wait("sync", "cV", 1)
                    p.wait("sync", "eV", quant_eV)
                    p.wait("sync", "qG", quant_qG)
                else:
                    prevkey = (l, pp - 1) if pp > 0 else (l - 1, 7)
                    p.wait("sync", "eT", q_marks[prevkey])
                inst = sy.dma_start(
                    out=wbuf[2][:, 0:1024],
                    in_=q_recv[258 * pp + 128 * l:258 * pp + 128 * (l + 1), :])
                qld = p.inc(inst, "qS", 16)
                for blk in range(nbB):
                    kc = blk // 4
                    bi = blk % 4
                    in_A = blk < nbA
                    sc = scps[blk % 2]
                    pb = pful[blk % 2]
                    if bi == 0:
                        if gc >= 3:
                            p.wait("sync", "eT", gc_marks[gc - 3])
                        inst = sy.dma_start(
                            out=wbuf[KVSLOT[gc % 3]][:, 0:512],
                            in_=k_recv[258 * (kc // 2) + 128 * l:
                                       258 * (kc // 2) + 128 * (l + 1),
                                       (kc % 2) * 512:(kc % 2) * 512 + 512])
                        p.inc(inst, "qS", 16)
                        inst = sy.dma_start(
                            out=wbuf[KVSLOT[gc % 3]][:, 512:1024]
                            .rearrange("p2 (tb d) -> p2 tb d", tb=4),
                            in_=v_recv[1024 * (kc // 2) + (kc % 2) * 512:
                                       1024 * (kc // 2) + (kc % 2) * 512 + 512,
                                       128 * l:128 * (l + 1)]
                            .rearrange("(tb p2) d -> p2 tb d", p2=128))
                        kvld = p.inc(inst, "qS", 16)
                        gc_ld = kvld
                    p.wait("tensor", "qS", gc_ld)
                    if blk == 0:
                        p.wait("tensor", "qS", qld)
                        if (l, pp) == (0, 0):
                            p.wait("tensor", "eV", quant_eV)
                    prev = None
                    if blk >= 2:
                        prev = (l, pp, blk - 2)
                    elif (l, pp) != (0, 0):
                        pk = (l, pp - 1) if pp > 0 else (l - 1, 7)
                        prev = (pk[0], pk[1], 4 * (2 * pk[1] + 2) - 2 + blk)
                    if prev is not None:
                        p.wait("tensor", "eS", exp_mk[prev])
                    klhs = wbuf[KVSLOT[gc % 3]][:, bi * 128:(bi + 1) * 128]
                    if in_A:
                        t.matmul(sc[:, 0:512], klhs, wbuf[2][:, 0:512],
                                 start=True, stop=True)
                    inst = t.matmul(sc[:, 512:1024], klhs, wbuf[2][:, 512:1024],
                                    start=True, stop=True)
                    qk = p.inc(inst, "eT", 1)
                    p.wait("scalar", "eT", qk)
                    if (l, pp, blk) == (0, 0, 0):
                        p.wait("scalar", "eV", negB_ev)
                    if prev is not None:
                        p.wait("scalar", "eT", pv_mk[prev])
                    if in_A:
                        inst = s.activation(out=pb[:, :], in_=sc[:, :],
                                            func=AF.Exp, scale=ISQ,
                                            bias=negB[:, l:l + 1])
                    else:
                        inst = s.activation(out=pb[:, 512:1024],
                                            in_=sc[:, 512:1024],
                                            func=AF.Exp, scale=ISQ,
                                            bias=negB[:, l:l + 1])
                    exp_mk[(l, pp, blk)] = p.inc(inst, "eS", 1)
                    mask_ev = None
                    mA = blk - (nbA - 4)
                    mB = blk - (nbB - 4)
                    if (in_A and 0 <= mA < 4) or 0 <= mB < 4:
                        p.wait("vector", "eS", exp_mk[(l, pp, blk)])
                        if in_A and 0 <= mA < 4:
                            inst = v.tensor_tensor(
                                out=pb[:, 0:512], in0=pb[:, 0:512],
                                in1=dmask[:, mA * 512:(mA + 1) * 512], op=OP.mult)
                        if 0 <= mB < 4:
                            inst = v.tensor_tensor(
                                out=pb[:, 512:1024], in0=pb[:, 512:1024],
                                in1=dmask[:, mB * 512:(mB + 1) * 512], op=OP.mult)
                        mask_ev = p.inc(inst, "eV", 1)
                    if mask_ev is not None:
                        p.wait("tensor", "eV", mask_ev)
                    else:
                        p.wait("tensor", "eS", exp_mk[(l, pp, blk)])
                    if blk == 0 and (l, pp) != (0, 0):
                        pk = (l, pp - 1) if pp > 0 else (l - 1, 7)
                        p.wait("tensor", "eV", ep_sc[pk + (0,)])
                        p.wait("tensor", "eV", ep_sc[pk + (1,)])
                        p.wait("tensor", "eV", ep_ev[pk + (0,)])
                        p.wait("tensor", "eV", ep_ev[pk + (1,)])
                    vlhs = wbuf[KVSLOT[gc % 3]][:, 512 + bi * 128:512 + (bi + 1) * 128]
                    if in_A:
                        t.matmul(yT[0][:, :], vlhs, pb[:, 0:512],
                                 start=(blk == 0), stop=(blk == nbA - 1))
                        t.matmul(smp[0][0:1, :], ones128[:, :], pb[:, 0:512],
                                 start=(blk == 0), stop=(blk == nbA - 1))
                    t.matmul(yT[1][:, :], vlhs, pb[:, 512:1024],
                             start=(blk == 0), stop=(blk == nbB - 1))
                    inst = t.matmul(smp[1][0:1, :], ones128[:, :],
                                    pb[:, 512:1024],
                                    start=(blk == 0), stop=(blk == nbB - 1))
                    pv_mk[(l, pp, blk)] = p.inc(inst, "eT", 1)
                    if DEBUG and (l, pp, blk) == (0, 0, 0):
                        p.wait("sync", "eT", pv_mk[(l, pp, blk)])
                        p.inc(sy.dma_start(out=dbg["dbg_p0"][:, :],
                                           in_=pful[0][:, :]), "qS", 16)
                    if bi == 3:
                        gc_marks.append(p.val("eT"))
                        gc += 1
                    if blk == nbB - 1:
                        q_marks[(l, pp)] = p.val("eT")
                    for ci in range(2):
                        if blk != (nbA - 1 if ci == 0 else nbB - 1):
                            continue
                        cglob = 2 * pp + ci
                        p.wait("vector", "eT", pv_mk[(l, pp, blk)])
                        vsync(v.reciprocal(out=rsum_sb[0:1, :],
                                           in_=smp[ci][0:1, :]))
                        rcp = vdup(lambda: v.tensor_copy(out=rs_bf[0:1, :],
                                                         in_=rsum_sb[0:1, :]))
                        p.wait("tensor", "eV", rcp)
                        inst = t.matmul(smp[ci][:, :], ones1x[:, :],
                                        rs_bf[0:1, :], start=True, stop=True)
                        rep = p.inc(inst, "eT", 1)
                        inst = v.tensor_copy(out=ysb_tmp[:, :],
                                             in_=yT[ci][:, :])
                        ep_sc[(l, pp, ci)] = p.inc(inst, "eV", 1)
                        p.wait("vector", "eT", rep)
                        if (l, cglob - 1) in yst_wr:
                            p.wait("vector", "qS", yst_wr[(l, cglob - 1)])
                        elif l == 1 and cglob == 0:
                            p.wait("vector", "qS", yst_wr[(0, 15)])
                        v.tensor_tensor(out=krt[:, 0:512],
                                        in0=ysb_tmp[:, :],
                                        in1=smp[ci][:, :], op=OP.mult)
                        inst = v.tensor_tensor(out=krt[:, 0:512],
                                               in0=ysb_tmp[:, :],
                                               in1=smp[ci][:, :], op=OP.mult)
                        ep_ev[(l, pp, ci)] = p.inc(inst, "eV", 1)
                        last_ep_ev = ep_ev[(l, pp, ci)]
                        if DEBUG and (l, pp, ci) == (0, 0, 0):
                            p.wait("sync", "eV", ep_sc[(l, pp, ci)])
                            p.inc(sy.dma_start(out=dbg["dbg_yu"][:, :],
                                               in_=ysb_tmp[:, :]), "qS", 16)
                            p.wait("sync", "eV", p.val("eV"))
                            p.inc(sy.dma_start(out=dbg["dbg_rs"][:, :],
                                               in_=rs_bf[0:1, :]), "qS", 16)
                        p.wait("sync", "eV", ep_ev[(l, pp, ci)])
                        inst = sy.dma_start(
                            out=y_src[256 * (cglob // 2) + 128 * l:
                                      256 * (cglob // 2) + 128 * (l + 1),
                                      (cglob % 2) * 512:(cglob % 2) * 512 + 512],
                            in_=krt[:, 0:512])
                        yst_wr[(l, cglob)] = p.inc(inst, "qS", 16)
                        if DEBUG and (l, pp, ci) == (0, 0, 0):
                            p.inc(sy.dma_start(out=dbg["dbg_y0"][:, :],
                                               in_=krt[:, 0:512]), "qS", 16)
        attn_eT = p.val("eT")
        attn_qS = p.val("qS")
        attn_eV = p.val("eV")
        attn_eS = p.val("eS")
        p.wait("gpsimd", "qS", attn_qS)
        with allow_ici_all_to_all():
            inst = gp.collective_compute(
                "AllToAll", OP.bypass, replica_groups=[list(range(8))],
                ins=[y_src.ap().opt()], outs=[y_recv.ap().opt()])
        p.inc(inst, "cY", 1)

        g_ld = {}
        g_et = {}
        sil_mk = {}
        h_ev = {}
        hq_ev = {}
        h_wr = {}
        sum_et = {}
        for gpi in range(16):
            w = gpi % 4
            r_g = gpi // 4
            r_u = 4 + gpi // 4
            for j in range(4):
                pi = 4 * gpi + j
                bslot = wbuf[(2 * pi) % 4]
                uslot = wbuf[(2 * pi + 1) % 4]
                p.wait("sync", f"cF{w}", 1)
                if pi == 0:
                    p.wait("sync", "eT", attn_eT)
                if pi >= 2:
                    p.wait("sync", "eT", g_et[pi - 2])
                inst = sy.dma_start(
                    out=bslot[:, :].rearrange("p2 (ec f) -> p2 ec f", ec=16),
                    in_=fw_dst[w][3072 * r_g:3072 * r_g + 2048,
                                  128 * j:128 * (j + 1)]
                    .rearrange("(ec p2) f -> p2 ec f", p2=128))
                p.inc(inst, "qS", 16)
                inst = sy.dma_start(
                    out=uslot[:, :].rearrange("p2 (ec f) -> p2 ec f", ec=16),
                    in_=fw_dst[w][3072 * r_u:3072 * r_u + 2048,
                                  128 * j:128 * (j + 1)]
                    .rearrange("(ec p2) f -> p2 ec f", p2=128))
                g_ld[pi] = p.inc(inst, "qS", 16)
                p.wait("tensor", "qS", g_ld[pi])
                if pi == 0:
                    p.wait("tensor", "eV", attn_eV)
                    p.wait("tensor", "eS", attn_eS)
                if pi >= 1:
                    p.wait("tensor", "eS", sil_mk[pi - 1])
                for ec in range(NEC):
                    lhsT = bslot[:, ec * 128:(ec + 1) * 128]
                    for th in range(2):
                        inst = t.matmul(
                            gA[:, th * 512:(th + 1) * 512], lhsT,
                            xn_fm[:, ec * 1024 + th * 512:
                                  ec * 1024 + (th + 1) * 512],
                            start=(ec == 0), stop=(ec == NEC - 1))
                gate_et = p.inc(inst, "eT", 1)
                if pi >= 2:
                    p.wait("tensor", "eV", h_ev[pi - 2])
                for ec in range(NEC):
                    lhsT = uslot[:, ec * 128:(ec + 1) * 128]
                    for th in range(2):
                        ups = (uB[:, th * 512:(th + 1) * 512] if pi % 2 == 0
                               else u45[th][:, :])
                        inst = t.matmul(
                            ups, lhsT,
                            xn_fm[:, ec * 1024 + th * 512:
                                  ec * 1024 + (th + 1) * 512],
                            start=(ec == 0), stop=(ec == NEC - 1))
                g_et[pi] = p.inc(inst, "eT", 1)
                p.wait("scalar", "eT", gate_et)
                if pi >= 1:
                    p.wait("scalar", "eV", h_ev[pi - 1])
                inst = s.activation(out=sgbuf[:, :], in_=gA[:, :], func=AF.Silu)
                sil_mk[pi] = p.inc(inst, "eS", 1)
                p.wait("vector", "eS", sil_mk[pi])
                p.wait("vector", "eT", g_et[pi])
                if pi >= 2:
                    p.wait("vector", "qS", h_wr[pi - 2])
                hb = hbuf[pi % 2]
                if pi % 2 == 0:
                    inst = v.tensor_tensor(out=hb[:, :], in0=sgbuf[:, :],
                                           in1=uB[:, :], op=OP.mult)
                else:
                    v.tensor_tensor(out=hb[:, 0:512], in0=sgbuf[:, 0:512],
                                    in1=u45[0][:, :], op=OP.mult)
                    inst = v.tensor_tensor(out=hb[:, 512:1024],
                                           in0=sgbuf[:, 512:1024],
                                           in1=u45[1][:, :], op=OP.mult)
                h_ev[pi] = p.inc(inst, "eV", 1)
                if pi >= 1:
                    p.wait("vector", "eT", sum_et[pi - 1])
                inst = v.tensor_tensor(out=hsqb[:, :], in0=hb[:, :],
                                       in1=hb[:, :], op=OP.mult)
                hq_ev[pi] = p.inc(inst, "eV", 1)
                p.wait("tensor", "eV", hq_ev[pi])
                for th in range(2):
                    inst = t.matmul(sq67[th][0:1, :], ones128[:, :],
                                    hsqb[:, th * 512:(th + 1) * 512],
                                    start=(pi == 0), stop=(pi == 63))
                sum_et[pi] = p.inc(inst, "eT", 1)
                p.wait("sync", "eV", h_ev[pi])
                inst = sy.dma_start(out=h_dram[128 * pi:128 * (pi + 1), :],
                                    in_=hb[:, :])
                h_wr[pi] = p.inc(inst, "qS", 16)
        gu_qS = p.val("qS")

        p.wait("vector", "eT", sum_et[63])
        for th in range(2):
            vsync(v.tensor_scalar(out=invr_row[0:1, th * 512:(th + 1) * 512],
                                  in0=sq67[th][0:1, :], scalar1=1.0 / H4,
                                  scalar2=RMS_EPS, op0=OP.mult, op1=OP.add))
            vsync(v.tensor_scalar(out=invr_row[0:1, th * 512:(th + 1) * 512],
                                  in0=invr_row[0:1, th * 512:(th + 1) * 512],
                                  scalar1=1e-12, scalar2=None, op0=OP.max))
        vdup(lambda: v.reciprocal(out=jnk[0:1, 0:1024], in_=invr_row[0:1, :]))
        p.wait("scalar", "eV", p.val("eV"))
        sdup(lambda: s.activation(out=invr_row[0:1, :], in_=jnk[0:1, 0:1024],
                                  func=AF.Sqrt))
        p.wait("vector", "eS", p.val("eS"))
        irv = vdup(lambda: v.tensor_copy(out=invr_bf[0:1, :],
                                         in_=invr_row[0:1, :]))
        p.wait("tensor", "eV", irv)
        for th in range(2):
            inst = t.matmul(prA[:, th * 512:(th + 1) * 512], ones1x[:, :],
                            invr_bf[0:1, th * 512:(th + 1) * 512],
                            start=True, stop=True)
        irt = p.inc(inst, "eT", 1)
        p.wait("vector", "eT", irt)
        inst = v.tensor_copy(out=invr_rep[:, :], in_=prA[:, :])
        ir_ev = p.inc(inst, "eV", 1)
        if DEBUG:
            p.wait("sync", "eV", ir_ev)
            p.inc(sy.dma_start(out=dbg["dbg_ir"][:, :], in_=invr_rep[:, :]), "qS", 16)
            p.wait("sync", "qS", gu_qS)
            p.inc(sy.dma_start(out=dbg["dbg_h"][:, :], in_=h_dram[:, :]), "qS", 16)

        p.wait("sync", "cY", 1)
        inst = sy.dma_start(out=y_all[:, :]
                            .rearrange("p2 (ec t2) -> p2 ec t2", ec=NEC),
                            in_=y_recv[:, :]
                            .rearrange("(ec p2) t2 -> p2 ec t2", p2=128))
        y_rl = p.inc(inst, "qS", 16)
        if DEBUG:
            p.wait("sync", "qS", y_rl)
            p.inc(sy.dma_start(out=dbg["dbg_y"][:, :], in_=y_all[:, :]), "qS", 16)
        pr_ld = [None] * 16
        pr_cast = [None] * 16
        pr_et = [None] * 16
        pr_ev = [None] * 16
        for fs in range(16):
            if fs >= 2:
                p.wait("sync", "eS", pr_cast[fs - 2])
            src = wprojT[:, 128 * fs:128 * (fs + 1)].rearrange(
                "(ec p2) f -> p2 ec f", p2=128)
            inst = sy.dma_start(
                out=xf[fs % 2][:, :].rearrange("p2 (ec f) -> p2 ec f", ec=16),
                in_=src)
            pr_ld[fs] = p.inc(inst, "qS", 16)
            p.wait("scalar", "qS", pr_ld[fs])
            if fs < 4:
                p.wait("scalar", "eT", g_et[63])
            else:
                p.wait("scalar", "eT", pr_et[fs - 4])
            inst = s.activation(out=wbuf[fs % 4][:, :], in_=xf[fs % 2][:, :],
                                func=AF.Copy)
            pr_cast[fs] = p.inc(inst, "eS", 1)
            ps = prA if fs % 2 == 0 else prB
            if fs >= 2:
                p.wait("tensor", "eV", pr_ev[fs - 2])
            if fs == 0:
                p.wait("tensor", "eV", ir_ev)
                p.wait("tensor", "qS", y_rl)
            p.wait("tensor", "eS", pr_cast[fs])
            for ec in range(NEC):
                lhsT = wbuf[fs % 4][:, ec * 128:(ec + 1) * 128]
                for th in range(2):
                    inst = t.matmul(
                        ps[:, th * 512:(th + 1) * 512], lhsT,
                        y_all[:, ec * 1024 + th * 512:ec * 1024 + (th + 1) * 512],
                        start=(ec == 0), stop=(ec == NEC - 1))
            pr_et[fs] = p.inc(inst, "eT", 1)
            p.wait("vector", "eT", pr_et[fs])
            inst = v.tensor_copy(out=xn_fm[:, fs * 1024:(fs + 1) * 1024],
                                 in_=ps[:, :])
            pr_ev[fs] = p.inc(inst, "eV", 1)
        s_eV = p.val("eV")
        if DEBUG:
            p.wait("sync", "eV", s_eV)
            p.inc(sy.dma_start(out=dbg["dbg_s"][:, :], in_=xn_fm[:, :]), "qS", 16)

        wp_ld = {}
        wp_et = {}
        h_ld = {}
        g_eS = {}
        for quad in range(4):
            p.wait("sync", f"cF{quad}", 1)
            for hc in range(64):
                e8 = hc // 8
                key = (quad, hc)
                if quad == 0 and hc == 0:
                    p.wait("sync", "eV", s_eV)
                    p.wait("sync", "qS", gu_qS)
                if hc >= 4 or quad > 0:
                    pk = (quad, hc - 4) if hc >= 4 else (quad - 1, 60 + hc)
                    p.wait("sync", "eT", wp_et[pk])
                inst = sy.dma_start(out=hbuf[hc % 4][:, :],
                                    in_=h_dram[128 * hc:128 * (hc + 1), :])
                h_ld[key] = p.inc(inst, "qS", 16)
                wps = []
                if hc == 0:
                    wps.append(0)
                elif hc % 8 == 1 and e8 < 7:
                    wps.append(e8 + 1)
                for e8x in wps:
                    for f in range(4):
                        base = 3072 * e8x + 2048
                        slot = wbuf[f][:, (e8x % 2) * 1024:(e8x % 2) * 1024 + 1024]
                        inst = sy.dma_start(
                            out=slot.rearrange("p2 (hcs f2) -> p2 hcs f2", hcs=8),
                            in_=fw_dst[quad][base:base + 1024,
                                             128 * f:128 * (f + 1)]
                            .rearrange("(hcs p2) f2 -> p2 hcs f2", p2=128))
                        p.inc(inst, "qS", 16)
                    wp_ld[(quad, e8x)] = p.val("qS")
                p.wait("tensor", "qS", h_ld[key])
                p.wait("tensor", "qS", wp_ld[(quad, e8)])
                if hc == 0:
                    if quad == 0:
                        p.wait("tensor", "eV", s_eV)
                    else:
                        for f in range(4):
                            p.wait("scalar", "eS", g_eS[(quad - 1, f)])
                        p.wait("tensor", "eS", g_eS[(quad - 1, 3)])
                for f in range(4):
                    lhsT = wbuf[f][:, (e8 % 2) * 1024 + (hc % 8) * 128:
                                   (e8 % 2) * 1024 + (hc % 8 + 1) * 128]
                    for th in range(2):
                        inst = t.matmul(
                            fp[f][:, th * 512:(th + 1) * 512], lhsT,
                            hbuf[hc % 4][:, th * 512:(th + 1) * 512],
                            start=(hc == 0), stop=(hc == 63))
                wp_et[key] = p.inc(inst, "eT", 1)
            for f in range(4):
                p.wait("vector", "eT", wp_et[(quad, 63)])
                p.wait("vector", "eV", ir_ev)
                if f >= 1:
                    p.wait("vector", "eS", g_eS[(quad, f - 1)])
                elif quad > 0:
                    p.wait("vector", "eS", g_eS[(quad - 1, 3)])
                inst = v.tensor_tensor(out=gt32[:, :], in0=fp[f][:, :],
                                       in1=invr_rep[:, :], op=OP.mult)
                gv = p.inc(inst, "eV", 1)
                p.wait("scalar", "eV", gv)
                inst = s.activation(
                    out=y_all[:, (4 * quad + f) * 1024:(4 * quad + f + 1) * 1024],
                    in_=gt32[:, :], func=AF.Silu)
                g_eS[(quad, f)] = p.inc(inst, "eS", 1)
        g_all_eS = p.val("eS")
        if DEBUG:
            p.wait("sync", "eS", g_all_eS)
            p.inc(sy.dma_start(out=dbg["dbg_g"][:, :], in_=y_all[:, :]), "qS", 16)

        x_rl = {}
        st_mk = {}
        tpw_ev = {}
        for tt in range(NTT):
            if tt >= 2:
                p.wait("sync", "qS", st_mk[tt - 2])
            inst = sy.dma_start(out=xf[tt % 2][:, :],
                                in_=x_in[128 * tt:128 * (tt + 1), :])
            x_rl[tt] = p.inc(inst, "qS", 16)
            wave_ev = []
            for wv in range(4):
                for k4 in range(4):
                    fc = 4 * wv + k4
                    if (tt, wv, k4) == (0, 0, 0):
                        p.wait("vector", "eS", g_all_eS)
                    p.wait("vector", "eT", p.val("eT"))
                    inst = v.tensor_tensor(
                        out=sgf[:, :],
                        in0=xn_fm[:, fc * 1024 + 128 * tt:
                                  fc * 1024 + 128 * (tt + 1)],
                        in1=y_all[:, fc * 1024 + 128 * tt:
                                  fc * 1024 + 128 * (tt + 1)],
                        op=OP.mult)
                    sgv = p.inc(inst, "eV", 1)
                    p.wait("tensor", "eV", sgv)
                    if k4 == 0:
                        wkey = (tt * 4 + wv) - 2
                        if wkey >= 0:
                            p.wait("tensor", "eV", tpw_ev[wkey])
                    inst = t.transpose(out=tp[wv % 2][:, k4 * 128:(k4 + 1) * 128],
                                       in_=sgf[:, :], identity=iden_f32[:, :])
                    p.inc(inst, "eT", 1)
                p.wait("vector", "eT", p.val("eT"))
                p.wait("vector", "qS", x_rl[tt])
                if tt >= 2 and wv == 0:
                    p.wait("vector", "qS", st_mk[tt - 2])
                inst = v.tensor_tensor(out=xf[2][:, 512 * wv:512 * (wv + 1)],
                                       in0=xf[tt % 2][:, 512 * wv:512 * (wv + 1)],
                                       in1=tp[wv % 2][:, :], op=OP.add)
                wave_ev.append(p.inc(inst, "eV", 1))
                tpw_ev[tt * 4 + wv] = wave_ev[-1]
            p.wait("sync", "eV", wave_ev[-1])
            inst = sy.dma_start(out=out_t[128 * tt:128 * (tt + 1), :],
                                in_=xf[2][:, :])
            st_mk[tt] = p.inc(inst, "qS", 16)
            if tt + 1 < NTT:
                p.wait("vector", "qS", st_mk[tt])
        p.wait("sync", "qS", p.val("qS"))

    with nc.Block() as blk:
        for name, starter in [("sync", blk.sync), ("gpsimd", blk.gpsimd),
                              ("scalar", blk.scalar), ("vector", blk.vector),
                              ("tensor", blk.tensor)]:
            def mk(nm):
                def f(eng):
                    emit(P(nc, sems, nm, eng))
                return f
            starter(mk(name))

    ctx.close()
    if sim:
        return nc.streams
    return nc


_CACHE = {}


def kernel(L0_attn_x, L0_attn_W_qkv, L0_attn_W_proj, L0_ffn_w_gate_up,
           L0_ffn_w_proj):
    x2d = np.ascontiguousarray(np.asarray(L0_attn_x, dtype=np.float32)
                               .reshape(T, D))
    wqkvT = np.ascontiguousarray(np.asarray(L0_attn_W_qkv, np.float32).T)
    wprojT = np.ascontiguousarray(np.asarray(L0_attn_W_proj, np.float32).T)
    guT = np.asarray(L0_ffn_w_gate_up, np.float32).T
    wpT = np.asarray(L0_ffn_w_proj, np.float32).T

    dmask_h = _diag_masks()
    iden_bf_h = np.eye(128, dtype=np.float32).astype(BF)
    iden_f32_h = np.eye(128, dtype=np.float32)
    gmat_h = np.zeros((128, 2), np.float32)
    gmat_h[0:64, 0] = 1.0
    gmat_h[64:128, 1] = 1.0
    gmat_h = gmat_h.astype(BF)
    gmat2_h = np.zeros((2, 128), np.float32)
    gmat2_h[0, 0:64] = 2.0
    gmat2_h[1, 64:128] = 2.0
    gmat2_h = gmat2_h.astype(BF)
    ones128_h = np.ones((128, 1), np.float32).astype(BF)
    ones1x_h = np.ones((1, 128), np.float32).astype(BF)

    if "nc" not in _CACHE:
        _CACHE["nc"] = build_program()
    nc = _CACHE["nc"]

    in_maps = []
    for i in range(8):
        pos = np.arange(TOWN * i, TOWN * (i + 1))
        C, S = _rope_tables(pos)
        in_maps.append({
            "x": np.ascontiguousarray(x2d[TOWN * i:TOWN * (i + 1)]),
            "wqkvT": wqkvT,
            "wprojT": wprojT,
            "gu_shard": np.ascontiguousarray(guT[:, 2048 * i:2048 * (i + 1)]),
            "wp_shard": np.ascontiguousarray(wpT[1024 * i:1024 * (i + 1), :]),
            "cosk": C, "sink": S,
            "dmask": dmask_h, "iden_bf": iden_bf_h, "iden_f32": iden_f32_h,
            "gmat": gmat_h, "gmat2": gmat2_h, "ones128c": ones128_h,
            "ones1x": ones1x_h,
        })

    res = run_bass_kernel_spmd(nc, in_maps, core_ids=list(range(8)))
    _CACHE["res"] = res
    out = np.empty((T, D), dtype=np.float32)
    for i in range(8):
        out[TOWN * i:TOWN * (i + 1), :] = res.results[i]["out"]
    return out.reshape(1, T, D)
